# version 04
# baseline (speedup 1.0000x reference)
import numpy as np
import concourse.bass as bass
import concourse.mybir as mybir
from concourse.bass_utils import run_bass_kernel_spmd
from concourse.ap import AP

F32 = mybir.dt.float32
BF16 = mybir.dt.bfloat16
AF = mybir.ActivationFunctionType
ALU = mybir.AluOpType

D = 1024
DI = 2048
DS = 128
H = 32
PH = 64
KC = 4
DTP = 4384
CH = 64
HD = 4096
NST = 12
EPS = 1e-5
CAP = 30.0
ZB = 16
XB = 18
NC_ = 8


class Buf:
    __slots__ = ("w", "r", "name")

    def __init__(self, name=""):
        self.w = None
        self.r = []
        self.name = name


class Prog:
    ENG = {"te": "tensor", "act": "scalar", "dve": "vector", "gps": "gpsimd",
           "sp": "sync"}
    DMAQ = ("sp", "gps", "act")
    NPOOL = 8

    def __init__(self, nc, block, sems):
        self.nc = nc
        self.block = block
        self.sems = sems
        self.cnt = {k: 0 for k in sems}
        self.waited = {e: {} for e in self.ENG}
        self.dmaslot = {q: 0 for q in self.DMAQ}

    def _emit(self, eng, fn, reads, writes, dma):
        toks = []
        for b in reads:
            toks.append(b.w)
        for b in writes:
            toks.append(b.w)
            toks.extend(b.r)
        need = {}
        for t in toks:
            if t is None:
                continue
            s, v = t
            if self.waited[eng].get(s, 0) < v:
                need[s] = max(need.get(s, 0), v)
        if dma:
            slot = self.dmaslot[eng]
            self.dmaslot[eng] = (slot + 1) % self.NPOOL
            semname = "D%s%d" % (eng, slot)
            prev = self.cnt[semname]
            if prev and self.waited[eng].get(semname, 0) < prev:
                need[semname] = max(need.get(semname, 0), prev)
            delta = 16
        else:
            semname = eng
            delta = 1
        self.cnt[semname] += delta
        tok = (semname, self.cnt[semname])
        sems = self.sems
        waited = self.waited[eng]

        def body(handle):
            for s, v in sorted(need.items()):
                handle.wait_ge(sems[s], v)
                waited[s] = max(waited.get(s, 0), v)
            fn(handle).then_inc(sems[semname], delta)

        getattr(self.block, self.ENG[eng])(body)
        for b in reads:
            b.r.append(tok)
        for b in writes:
            b.w = tok
            b.r = []
        return tok

    def op(self, eng, meth, *args, reads=(), writes=(), dma=False, **kw):
        return self._emit(eng, lambda h: getattr(h, meth)(*args, **kw),
                          reads, writes, dma)

    def raw(self, eng, fn, reads=(), writes=(), dma=False):
        return self._emit(eng, fn, reads, writes, dma)

    def barrier(self):
        toks = [(s, v) for s, v in self.cnt.items() if v > 0]
        for eng in self.ENG:
            self.fence(eng, [(s, v) for s, v in toks if s != eng])

    def fence(self, eng, toks):
        need = {}
        for t in toks:
            if t is None:
                continue
            s, v = t
            if self.waited[eng].get(s, 0) < v:
                need[s] = max(need.get(s, 0), v)
        if not need:
            return
        sems = self.sems
        waited = self.waited[eng]

        def body(handle):
            for s, v in sorted(need.items()):
                handle.wait_ge(sems[s], v)
                waited[s] = max(waited.get(s, 0), v)

        getattr(self.block, self.ENG[eng])(body)


def build_core_program(NT=4096, TA=256, TB=256, TC=256, n_cores=NC_,
                       exchange="cc", debug_dump=()):
    dbgset = set(debug_dump)
    NCH = NT // CH
    nc = bass.Bass("TRN2", target_bir_lowering=False, num_devices=n_cores)

    def ein(name, shape, dt=F32):
        return nc.dram_tensor(name, shape, dt, kind="ExternalInput").ap()

    x_in = ein("x", [NT, D])
    xh_in = ein("xh", [4, D])
    w_in_h = ein("w_in", [D, DTP])
    conv_w_h = ein("conv_w", [128, KC * XB])
    conv_b_h = ein("conv_b", [128, XB])
    dt_bias_h = ein("dt_bias", [H, 1])
    a_log_h = ein("a_log", [H, 1])
    d_h = ein("dvec", [1, H])
    norm_w_h = ein("norm_w", [128, ZB])
    w_out_h = ein("w_out", [DI, D])
    w_expT_h = ein("w_expT", [HD, D])
    w_conP_h = ein("w_conP", [HD, D])
    b_exp2_h = ein("b_exp2", [128, 32])
    up_bias_p_h = ein("up_bias_p", [128, HD // 128])
    b_out_h = ein("b_out", [1, D])
    down_b_h = ein("down_b", [1, D])
    bfv_h = ein("bfv", [128, 8 * NST * 32], BF16)
    ident_h = ein("ident", [128, 128], BF16)
    xpat_h = ein("xpat", [128, 8 * 128], BF16)
    tril_h = ein("tril", [128, CH])
    tb2_h = ein("tb2", [2, 128])
    ohE_h = ein("ohE", [H, 16 * 128])
    smask_h = ein("smask", [128, 8])

    out_ap = nc.dram_tensor("out", [NT, D], F32, kind="ExternalOutput").ap()

    def dscr(name, shape, dt):
        return nc.dram_tensor(name, shape, dt).ap()

    sp_z = dscr("sp_z", [DI, NT], BF16)
    sp_xs = dscr("sp_xs", [NT, DI], BF16)
    sp_xsw = dscr("sp_xsw", [NT, DI], BF16)
    sp_bfm = dscr("sp_bfm", [DS, NT], BF16)
    sp_btm = dscr("sp_btm", [NT, DS], BF16)
    sp_cfm = dscr("sp_cfm", [DS, NT], BF16)
    sp_x1 = dscr("sp_x1", [NT, D], F32)
    sp_w = dscr("sp_w", [HD, D], BF16)
    sp_w2 = dscr("sp_w2", [HD, D], BF16)
    sp_g2 = dscr("sp_g2", [H, 512], F32)
    sp_bv = dscr("sp_bv", [HD, 1], F32)
    cc_in = dscr("cc_in", [DS, DI], F32)
    cc_out = dscr("cc_out", [n_cores * DS, DI], F32)
    bS = {}
    for _n in ("sp_z", "sp_xs", "sp_xsw", "sp_bfm", "sp_btm", "sp_cfm",
               "sp_x1", "sp_w", "sp_w2", "sp_g2", "sp_bv", "cc_in", "cc_out"):
        bS[_n] = Buf(_n)
    if exchange == "none":
        sin_h = ein("sin", [DS, DI], F32)
        sfin_t = nc.dram_tensor("sfin", [DS, DI], F32, kind="ExternalOutput").ap()

    import contextlib
    with contextlib.ExitStack() as _es:
        sems = {}
        for e in ("te", "act", "dve", "gps", "sp"):
            sems[e] = _es.enter_context(nc.semaphore("s_" + e))
        for q in ("sp", "gps", "act"):
            for i in range(Prog.NPOOL):
                sems["D%s%d" % (q, i)] = _es.enter_context(
                    nc.semaphore("sD%s%d" % (q, i)))
        block = _es.enter_context(nc.Block())
        p = Prog(nc, block, sems)

        def sb(name, shape, dt, stack=None):
            if stack is None:
                t = nc.alloc_sbuf_tensor(name, shape, dt)
            else:
                t = stack.enter_context(nc.sbuf_tensor(name, shape, dt))
            return t, Buf(name)

        cid, b_cid = sb("c_id", [128, 128], BF16)
        cidf, b_cidf = sb("c_idf", [128, 128], F32)
        co1, b_co1 = sb("c_o1", [1, 128], BF16)
        co1f, b_co1f = sb("c_o1f", [1, 128], F32)
        coK, b_coK = sb("c_oK", [128, 1], BF16)
        stat, b_stat = sb("stat", [128, 24], F32)
        rowa, b_rowa = sb("rowa", [1, 64], F32)
        boutsb, b_bout = sb("boutsb", [1, D], BF16)
        _esM = contextlib.ExitStack()
        ctril, b_ctril = sb("c_tril", [128, CH], F32, _esM)
        ctb2, b_ctb2 = sb("c_tb2", [2, 128], F32, _esM)
        czs, b_czs = sb("c_zs", [H, 512], F32, _esM)
        convw, b_convw = sb("p_cw", [128, KC * XB], F32, _esM)
        convb, b_convb = sb("p_cb", [128, XB], F32, _esM)
        dtbias, b_dtbias = sb("p_dtb", [H, 1], F32, _esM)
        aneg, b_aneg = sb("p_aneg", [H, 1], F32, _esM)
        dvecs, b_dvecs = sb("p_dv", [1, H], F32, _esM)
        normw, b_normw = sb("p_nw", [128, ZB], F32, _esM)
        smk, b_smk = sb("p_smk", [128, 8], F32, _esM)
        gfm, b_gfm = sb("gfm", [H, NT], F32, _esM)
        lndt, b_lndt = sb("lndt", [H, NT], F32, _esM)
        dcrow, b_dcrow = sb("dcrow", [1, NCH * H], F32, _esM)
        gcar, b_gcar = sb("gcar", [H, 1], F32, _esM)
        dcprev, b_dcprev = sb("dcprev", [H, 1], F32, _esM)
        S, b_S = sb("S", [DS, DI], F32, _esM)
        Sb_, b_Sb = sb("Sb", [DS, DI], BF16, _esM)
        St, b_St = sb("St", [DS, DI], F32, _esM)
        ddiag, b_ddiag = sb("ddiag", [128, H * CH], BF16, _esM)
        drep, b_drep = sb("drep", [128, H], F32, _esM)
        rowz, b_rowz = sb("rowz", [1, 1024], BF16, _esM)

        ps0 = nc.alloc_psum_tensor("ps0", [128, 1024], F32)
        ps1 = nc.alloc_psum_tensor("ps1", [128, 1024], F32)
        ps2 = nc.alloc_psum_tensor("ps2", [128, 1024], F32)
        pt0 = nc.alloc_psum_tensor("pt0", [128, 1024], BF16)
        pt1 = nc.alloc_psum_tensor("pt1", [128, 1024], BF16)
        b_ps0, b_ps1, b_ps2 = Buf("ps0"), Buf("ps1"), Buf("ps2")
        b_pt0, b_pt1 = Buf("pt0"), Buf("pt1")
        PT = [(pt0, b_pt0), (pt1, b_pt1)]

        def ld(dst, bdst, src):
            p.op("gps", "dma_start", out=dst[:], in_=src, writes=[bdst], dma=True)

        ld(cid, b_cid, ident_h)
        p.op("dve", "tensor_copy", out=cidf[:], in_=cid[:], reads=[b_cid],
             writes=[b_cidf])
        ld(ctril, b_ctril, tril_h)
        ld(ctb2, b_ctb2, tb2_h)
        ld(convw, b_convw, conv_w_h)
        ld(convb, b_convb, conv_b_h)
        ld(dtbias, b_dtbias, dt_bias_h)
        ld(aneg, b_aneg, a_log_h)
        ld(dvecs, b_dvecs, d_h)
        ld(normw, b_normw, norm_w_h)
        ld(smk, b_smk, smask_h)
        p.op("dve", "memset", co1[:], 1.0, writes=[b_co1])
        p.op("dve", "memset", co1f[:], 1.0, writes=[b_co1f])
        p.op("dve", "memset", coK[:], 1.0, writes=[b_coK])
        p.op("dve", "memset", czs[:], 0.0, writes=[b_czs])
        p.op("dve", "memset", gcar[:], 0.0, writes=[b_gcar])
        p.op("dve", "memset", dcprev[:], 0.0, writes=[b_dcprev])
        p.op("dve", "memset", S[:], 0.0, writes=[b_S])
        p.op("act", "activation", aneg[:], aneg[:], AF.Exp,
             reads=[b_aneg], writes=[b_aneg])
        p.op("dve", "tensor_scalar_mul", aneg[:], aneg[:], -1.0,
             reads=[b_aneg], writes=[b_aneg])
        p.op("gps", "dma_start", out=boutsb[:], in_=b_out_h, writes=[b_bout], dma=True)
        p.op("gps", "dma_start", out=rowz[:], in_=down_b_h, writes=[b_rowz], dma=True)
        p.op("dve", "tensor_add", out=boutsb[:], in0=boutsb[:], in1=rowz[:],
             reads=[b_rowz], writes=[b_bout])
        p.raw("te", lambda t: t.matmul(
            ps2[:, 0:H], co1f[:], dvecs[0:1, 0:H],
            start=True, stop=True),
            reads=[b_co1f, b_dvecs], writes=[b_ps2])
        p.op("dve", "tensor_copy", out=drep[:], in_=ps2[:, 0:H],
             reads=[b_ps2], writes=[b_drep])
        p.raw("dve", lambda v: v.tensor_tensor(
            out=AP(ddiag, 0, [[H * CH, 64], [CH, H], [1, CH]]),
            in0=AP(cid, 7 * 0 + 0, [[128, 64], [0, H], [1, 64]]),
            in1=AP(drep, 0, [[H, 64], [1, H], [0, CH]]), op=ALU.mult),
            reads=[b_cid, b_drep], writes=[b_ddiag])
        p.raw("dve", lambda v: v.tensor_tensor(
            out=AP(ddiag, 64 * (H * CH), [[H * CH, 64], [CH, H], [1, CH]]),
            in0=AP(cid, 64 * 128 + 64, [[128, 64], [0, H], [1, 64]]),
            in1=AP(drep, 64 * H, [[H, 64], [1, H], [0, CH]]), op=ALU.mult),
            reads=[b_cid, b_drep], writes=[b_ddiag])

        def ln_tm(src, b_src, dst, b_dst, np_, width, soff):
            sums = stat[0:np_, 0:1]
            ssq = stat[0:np_, 1:2]
            mu = stat[0:np_, 2:3]
            ivar = stat[0:np_, 3:4]
            scr = stat[0:np_, 4:5]
            p.op("dve", "reduce_sum", sums, src[0:np_, soff:soff + width],
                 axis=mybir.AxisListType.X, reads=[b_src], writes=[b_stat])
            p.op("act", "activation", dst[0:np_, soff:soff + width],
                 src[0:np_, soff:soff + width], AF.Square, accum_out=ssq,
                 reads=[b_src], writes=[b_dst, b_stat])
            inv_n = 1.0 / width
            p.op("dve", "tensor_scalar_mul", mu, sums, inv_n,
                 reads=[b_stat], writes=[b_stat])
            p.op("dve", "tensor_mul", out=scr, in0=mu, in1=mu,
                 reads=[b_stat], writes=[b_stat])
            p.op("dve", "tensor_scalar_mul", ivar, ssq, inv_n,
                 reads=[b_stat], writes=[b_stat])
            p.op("dve", "tensor_sub", out=ivar, in0=ivar, in1=scr,
                 reads=[b_stat], writes=[b_stat])
            p.op("dve", "tensor_scalar_add", ivar, ivar, EPS,
                 reads=[b_stat], writes=[b_stat])
            p.op("act", "activation", ivar, ivar, AF.Ln,
                 reads=[b_stat], writes=[b_stat])
            p.op("act", "activation", ivar, ivar, AF.Exp, scale=-0.5,
                 reads=[b_stat], writes=[b_stat])
            p.op("dve", "tensor_scalar", out=dst[0:np_, soff:soff + width],
                 in0=src[0:np_, soff:soff + width], scalar1=mu, scalar2=ivar,
                 op0=ALU.subtract, op1=ALU.mult,
                 reads=[b_src, b_stat], writes=[b_dst])

        NTT_A = NT // TA
        nblk = TA // 128
        with contextlib.ExitStack() as _esA:
            winsb = _esA.enter_context(nc.sbuf_tensor("winsb", [128, 8 * DTP], BF16))
            xt = _esA.enter_context(nc.sbuf_tensor("xt", [128, nblk * D], F32))
            xnt = _esA.enter_context(nc.sbuf_tensor("xnt", [128, nblk * D], BF16))
            xnf = _esA.enter_context(nc.sbuf_tensor("xnf", [128, 8 * TA], BF16))
            zsl = _esA.enter_context(nc.sbuf_tensor("zsl", [128, ZB * TA], BF16))
            cvo = _esA.enter_context(nc.sbuf_tensor("cvo", [128, TA], F32))
            xsf = _esA.enter_context(nc.sbuf_tensor("xsf", [128, 16 * TA], BF16))
            bcf = _esA.enter_context(nc.sbuf_tensor("bcf", [128, 2 * TA], BF16))
            xtl = _esA.enter_context(nc.sbuf_tensor("xtl", [128, XB * 3], F32))
            xstm = _esA.enter_context(nc.sbuf_tensor("xstm", [128, nblk * DI], BF16))
            xswt = _esA.enter_context(nc.sbuf_tensor("xswt", [128, nblk * DI], BF16))
            btm = _esA.enter_context(nc.sbuf_tensor("btm", [128, nblk * DS], BF16))
            dts = _esA.enter_context(nc.sbuf_tensor("dts", [H, TA * 4 + 256], F32))
            wtp = _esA.enter_context(nc.sbuf_tensor("wtp", [128, nblk * H], F32))
            dctt = _esA.enter_context(nc.sbuf_tensor("dctt", [8, H], F32))
            b_win, b_xt, b_xnt, b_xnf = Buf("win"), Buf("xt"), Buf("xnt"), Buf("xnf")
            b_zsl, b_cvo, b_xsf = Buf("zsl"), Buf("cvo"), Buf("xsf")
            b_bcf, b_xtl = Buf("bcf"), Buf("xtl")
            b_xstm, b_xswt, b_btm = Buf("xstm"), Buf("xswt"), Buf("btm")
            b_dts, b_wtp, b_dctt = Buf("dts"), Buf("wtp"), Buf("dctt")

            for k in range(8):
                p.raw("gps", lambda g, k=k: g.dma_start(
                    out=winsb[:, k * DTP:(k + 1) * DTP],
                    in_=w_in_h[k * 128:(k + 1) * 128, :]),
                    writes=[b_win], dma=True)

            p.op("sp", "dma_start", out=xt[0:4, 0:D], in_=xh_in,
                 writes=[b_xt], dma=True)
            ln_tm(xt, b_xt, xnt, b_xnt, 4, D, 0)
            for k in range(8):
                pt, b_pt = PT[k % 2]
                p.op("dve", "tensor_copy", out=xnf[0:4, k * TA:k * TA + 128],
                     in_=xnt[0:4, k * 128:(k + 1) * 128],
                     reads=[b_xnt], writes=[b_xnf])
                p.op("te", "transpose", pt[:, 0:4],
                     xnf[0:4, k * TA:k * TA + 128], cid[0:4, 0:4],
                     reads=[b_xnf, b_cid], writes=[b_pt])
                p.op("dve", "tensor_copy", out=xnf[:, k * TA + TA - 4:k * TA + TA],
                     in_=pt[:, 0:4], reads=[b_pt], writes=[b_xnf])
            for fb in range(XB):
                fc = DI + fb * 128
                for k in range(8):
                    p.op("te", "matmul", ps0[:, 0:4],
                         winsb[:, k * DTP + fc:k * DTP + fc + 128],
                         xnf[:, k * TA + TA - 4:k * TA + TA],
                         start=(k == 0), stop=(k == 7),
                         reads=[b_win, b_xnf], writes=[b_ps0])
                p.op("dve", "tensor_copy", out=xtl[:, fb * 3:fb * 3 + 3],
                     in_=ps0[:, 0:3], reads=[b_ps0], writes=[b_xtl])

            for it in range(NTT_A):
                t0 = it * TA
                for j in range(nblk):
                    p.raw("sp", lambda s, j=j, t0=t0: s.dma_start(
                        out=xt[:, j * D:(j + 1) * D],
                        in_=x_in[t0 + j * 128:t0 + (j + 1) * 128, :]),
                        writes=[b_xt], dma=True)
                for j in range(nblk):
                    ln_tm(xt, b_xt, xnt, b_xnt, 128, D, j * D)
                for j in range(nblk):
                    for k in range(8):
                        pt, b_pt = PT[(j * 8 + k) % 2]
                        p.op("te", "transpose", pt[:, 0:128],
                             xnt[:, j * D + k * 128:j * D + (k + 1) * 128],
                             cid[:], reads=[b_xnt, b_cid], writes=[b_pt])
                        p.op("dve", "tensor_copy",
                             out=xnf[:, k * TA + j * 128:k * TA + (j + 1) * 128],
                             in_=pt[:, 0:128], reads=[b_pt], writes=[b_xnf])
                slots = [0, 512]
                si = 0
                for fb in range(ZB):
                    off = slots[si]; si ^= 1
                    for k in range(8):
                        p.op("te", "matmul", ps0[:, off:off + TA],
                             winsb[:, k * DTP + fb * 128:k * DTP + (fb + 1) * 128],
                             xnf[:, k * TA:(k + 1) * TA],
                             start=(k == 0), stop=(k == 7),
                             reads=[b_win, b_xnf], writes=[b_ps0])
                    p.op("act", "activation", zsl[:, fb * TA:(fb + 1) * TA],
                         ps0[:, off:off + TA], AF.Silu,
                         reads=[b_ps0], writes=[b_zsl])
                for fb in range(XB):
                    fc = DI + fb * 128
                    off = slots[si]; si ^= 1
                    for k in range(8):
                        p.op("te", "matmul", ps0[:, off:off + TA],
                             winsb[:, k * DTP + fc:k * DTP + fc + 128],
                             xnf[:, k * TA:(k + 1) * TA],
                             start=(k == 0), stop=(k == 7),
                             reads=[b_win, b_xnf], writes=[b_ps0])
                    wcol = lambda k_: convw[:, k_ * XB + fb:k_ * XB + fb + 1]
                    p.op("dve", "tensor_scalar_mul", cvo[:], ps0[:, off:off + TA],
                         wcol(3), reads=[b_ps0, b_convw], writes=[b_cvo])
                    for k in range(3):
                        sh = 3 - k
                        p.op("dve", "scalar_tensor_tensor",
                             out=cvo[:, sh:TA], in0=ps0[:, off:off + TA - sh],
                             scalar=wcol(k), in1=cvo[:, sh:TA],
                             op0=ALU.mult, op1=ALU.add,
                             reads=[b_ps0, b_convw, b_cvo], writes=[b_cvo])
                        p.op("dve", "scalar_tensor_tensor",
                             out=cvo[:, 0:sh],
                             in0=xtl[:, fb * 3 + 3 - sh:fb * 3 + 3],
                             scalar=wcol(k), in1=cvo[:, 0:sh],
                             op0=ALU.mult, op1=ALU.add,
                             reads=[b_xtl, b_convw, b_cvo], writes=[b_cvo])
                    p.op("dve", "tensor_copy", out=xtl[:, fb * 3:fb * 3 + 3],
                         in_=ps0[:, off + TA - 3:off + TA],
                         reads=[b_ps0], writes=[b_xtl])
                    dst = (xsf[:, fb * TA:(fb + 1) * TA] if fb < 16 else
                           bcf[:, (fb - 16) * TA:(fb - 15) * TA])
                    wdst = b_xsf if fb < 16 else b_bcf
                    p.op("act", "activation", dst, cvo[:], AF.Silu,
                         bias=convb[:, fb:fb + 1],
                         reads=[b_cvo, b_convb], writes=[wdst])
                offd = slots[si]; si ^= 1
                for k in range(8):
                    p.op("te", "matmul", ps0[0:H, offd:offd + TA],
                         winsb[:, k * DTP + 4352:k * DTP + 4384],
                         xnf[:, k * TA:(k + 1) * TA],
                         start=(k == 0), stop=(k == 7),
                         reads=[b_win, b_xnf], writes=[b_ps0])
                u = dts[:, 0:TA]
                v1 = dts[:, TA:2 * TA]
                w2 = dts[:, 2 * TA:3 * TA]
                dtv = dts[:, 3 * TA:4 * TA]
                xtr = dts[:, 4 * TA:4 * TA + 256]
                p.op("dve", "tensor_scalar_add", u, ps0[0:H, offd:offd + TA],
                     dtbias[:], reads=[b_ps0, b_dtbias], writes=[b_dts])
                p.op("act", "activation", v1, u, AF.Abs,
                     reads=[b_dts], writes=[b_dts])
                p.op("act", "activation", v1, v1, AF.Exp, scale=-1.0,
                     reads=[b_dts], writes=[b_dts])
                p.op("act", "activation", v1, v1, AF.Ln, bias=1.0,
                     reads=[b_dts], writes=[b_dts])
                p.op("act", "activation", w2, u, AF.Relu,
                     reads=[b_dts], writes=[b_dts])
                p.op("dve", "tensor_add", out=dtv, in0=v1, in1=w2,
                     reads=[b_dts], writes=[b_dts])
                p.op("act", "activation", lndt[:, t0:t0 + TA], dtv, AF.Ln,
                     reads=[b_dts], writes=[b_lndt])
                p.op("dve", "tensor_scalar_mul", u, dtv, aneg[:],
                     reads=[b_dts, b_aneg], writes=[b_dts])
                p.op("dve", "tensor_tensor_scan", out=gfm[:, t0:t0 + TA],
                     data0=u, data1=czs[:, 0:TA], initial=gcar[:],
                     op0=ALU.add, op1=ALU.add,
                     reads=[b_dts, b_czs, b_gcar], writes=[b_gfm])
                p.op("dve", "tensor_copy", out=gcar[:],
                     in_=gfm[:, t0 + TA - 1:t0 + TA],
                     reads=[b_gfm], writes=[b_gcar])
                nct = TA // CH
                c0 = t0 // CH
                p.op("dve", "tensor_copy", out=xtr[:, 0:1], in_=dcprev[:],
                     reads=[b_dcprev], writes=[b_dts])
                if nct > 1:
                    p.raw("dve", lambda v, t0=t0, nct=nct: v.tensor_copy(
                        out=xtr[:, 1:nct],
                        in_=AP(gfm, t0 + CH - 1, [[NT, H], [CH, nct - 1]])),
                        reads=[b_gfm], writes=[b_dts])
                p.raw("dve", lambda v, t0=t0, nct=nct: v.tensor_sub(
                    out=xtr[:, 8:8 + nct],
                    in0=AP(gfm, t0 + CH - 1, [[NT, H], [CH, nct]]),
                    in1=xtr[:, 0:nct]),
                    reads=[b_gfm, b_dts], writes=[b_dts])
                p.op("act", "activation", xtr[:, 8:8 + nct], xtr[:, 8:8 + nct],
                     AF.Exp, reads=[b_dts], writes=[b_dts])
                p.raw("dve", lambda v, t0=t0: v.tensor_copy(
                    out=dcprev[:], in_=AP(gfm, t0 + TA - 1, [[NT, H], [1, 1]])),
                    reads=[b_gfm], writes=[b_dcprev])
                p.op("te", "transpose", ps2[0:nct, 896:896 + H],
                     xtr[:, 8:8 + nct], cidf[0:H, 0:H],
                     reads=[b_dts, b_cidf], writes=[b_ps2])
                p.op("dve", "tensor_copy", out=dctt[0:nct, :],
                     in_=ps2[0:nct, 896:896 + H], reads=[b_ps2], writes=[b_dctt])
                p.raw("gps", lambda g, c0=c0, nct=nct: g.dma_start(
                    out=dcrow[0:1, c0 * H:(c0 + nct) * H],
                    in_=dctt[0:nct, :]),
                    reads=[b_dctt], writes=[b_dcrow], dma=True)
                p.raw("dve", lambda v, t0=t0, nct=nct: v.tensor_sub(
                    out=w2,
                    in0=AP(gfm, t0 + CH - 1, [[NT, H], [CH, nct], [0, CH]]),
                    in1=gfm[:, t0:t0 + TA]),
                    reads=[b_gfm], writes=[b_dts])
                p.op("act", "activation", w2, w2, AF.Exp,
                     reads=[b_dts], writes=[b_dts])
                p.op("dve", "tensor_mul", out=w2, in0=w2, in1=dtv,
                     reads=[b_dts], writes=[b_dts])

                for j in range(nblk):
                    for cb in range(16):
                        pt, b_pt = PT[cb % 2]
                        p.op("te", "transpose", pt[:, 0:128],
                             xsf[:, cb * TA + j * 128:cb * TA + (j + 1) * 128],
                             cid[:], reads=[b_xsf, b_cid], writes=[b_pt])
                        p.op("dve", "tensor_copy",
                             out=xstm[:, j * DI + cb * 128:j * DI + (cb + 1) * 128],
                             in_=pt[:, 0:128], reads=[b_pt], writes=[b_xstm])
                    pt, b_pt = PT[0]
                    p.op("te", "transpose", pt[:, 0:128],
                         bcf[:, j * 128:(j + 1) * 128], cid[:],
                         reads=[b_bcf, b_cid], writes=[b_pt])
                    p.op("dve", "tensor_copy", out=btm[:, j * DS:(j + 1) * DS],
                         in_=pt[:, 0:128], reads=[b_pt], writes=[b_btm])
                    p.op("te", "transpose", ps2[0:128, 896:896 + H],
                         w2[:, j * 128:(j + 1) * 128], cidf[0:H, 0:H],
                         reads=[b_dts, b_cidf], writes=[b_ps2])
                    p.op("dve", "tensor_copy", out=wtp[:, j * H:(j + 1) * H],
                         in_=ps2[0:128, 896:896 + H], reads=[b_ps2], writes=[b_wtp])
                    p.raw("dve", lambda v, j=j: v.tensor_tensor(
                        out=xswt[:, j * DI:(j + 1) * DI],
                        in0=xstm[:, j * DI:(j + 1) * DI],
                        in1=AP(wtp, j * H, [[nblk * H, 128], [1, H], [0, PH]]),
                        op=ALU.mult),
                        reads=[b_xstm, b_wtp], writes=[b_xswt])
                    for cc in range(2):
                        c_glob = (t0 + j * 128) // CH + cc
                        for q in range(4):
                            pstt = ps1 if q < 2 else ps2
                            b_pstt = b_ps1 if q < 2 else b_ps2
                            p.op("te", "matmul",
                                 pstt[:, (q % 2) * 512:(q % 2 + 1) * 512],
                                 btm[cc * 64:(cc + 1) * 64, j * DS:(j + 1) * DS],
                                 xswt[cc * 64:(cc + 1) * 64,
                                      j * DI + q * 512:j * DI + (q + 1) * 512],
                                 start=True, stop=True, tile_position=(cc * 64, 0),
                                 reads=[b_btm, b_xswt], writes=[b_pstt])
                        p.raw("te", lambda t, c_glob=c_glob: t.matmul(
                            ps0[:, 0:H], co1f[:],
                            dcrow[0:1, c_glob * H:(c_glob + 1) * H],
                            start=True, stop=True),
                            reads=[b_co1f, b_dcrow], writes=[b_ps0])
                        p.raw("dve", lambda v: v.tensor_tensor(
                            out=St[:], in0=S[:],
                            in1=AP(ps0, 0, [[1024, 128], [1, H], [0, PH]]),
                            op=ALU.mult),
                            reads=[b_S, b_ps0], writes=[b_St])
                        p.raw("dve", lambda v: v.tensor_add(
                            out=S[:, 0:1024], in0=St[:, 0:1024], in1=ps1[:, :]),
                            reads=[b_St, b_ps1], writes=[b_S])
                        p.raw("dve", lambda v: v.tensor_add(
                            out=S[:, 1024:2048], in0=St[:, 1024:2048], in1=ps2[:, :]),
                            reads=[b_St, b_ps2], writes=[b_S])
                    gt = t0 + j * 128
                    p.raw("sp", lambda s, j=j, gt=gt: s.dma_start(
                        out=sp_xs[gt:gt + 128, :],
                        in_=xstm[:, j * DI:(j + 1) * DI]),
                        reads=[b_xstm], writes=[bS["sp_xs"]], dma=True)
                    p.raw("sp", lambda s, j=j, gt=gt: s.dma_start(
                        out=sp_xsw[gt:gt + 128, :],
                        in_=xswt[:, j * DI:(j + 1) * DI]),
                        reads=[b_xswt], writes=[bS["sp_xsw"]], dma=True)
                    p.raw("act", lambda s, j=j, gt=gt: s.dma_start(
                        out=sp_btm[gt:gt + 128, :],
                        in_=btm[:, j * DS:(j + 1) * DS]),
                        reads=[b_btm], writes=[bS["sp_btm"]], dma=True)
                p.raw("act", lambda s, t0=t0: s.dma_start(
                    out=AP(sp_z.tensor, t0, [[NT, 128], [128 * NT, ZB], [1, TA]]),
                    in_=zsl[:]), reads=[b_zsl], writes=[bS["sp_z"]], dma=True)
                p.raw("act", lambda s, t0=t0: s.dma_start(
                    out=sp_bfm[:, t0:t0 + TA], in_=bcf[:, 0:TA]),
                    reads=[b_bcf], writes=[bS["sp_bfm"]], dma=True)
                p.raw("act", lambda s, t0=t0: s.dma_start(
                    out=sp_cfm[:, t0:t0 + TA], in_=bcf[:, TA:2 * TA]),
                    reads=[b_bcf], writes=[bS["sp_cfm"]], dma=True)

        p.barrier()
        if exchange == "cc":
            p.op("sp", "dma_start", out=cc_in, in_=S[:], reads=[b_S],
                 writes=[bS["cc_in"]], dma=True)
            b_cc = Buf("cc")
            p.raw("gps", lambda g: g.collective_compute(
                "AllGather", ALU.bypass, replica_groups=[list(range(n_cores))],
                ins=[cc_in], outs=[cc_out]),
                reads=[bS["cc_in"]], writes=[bS["cc_out"]])
            p.op("dve", "memset", S[:], 0.0, writes=[b_S])
            gsl, b_gsl = sb("gslot", [DS, DI], F32, _esM)
            for k in range(n_cores):
                p.raw("sp", lambda s, k=k: s.dma_start(
                    out=gsl[:], in_=cc_out[k * DS:(k + 1) * DS, :]),
                    reads=[bS["cc_out"]], writes=[b_gsl], dma=True)
                p.op("dve", "scalar_tensor_tensor", out=S[:], in0=gsl[:],
                     scalar=smk[:, k:k + 1], in1=S[:], op0=ALU.mult, op1=ALU.add,
                     reads=[b_gsl, b_smk], writes=[b_S])
        else:
            p.op("sp", "dma_start", out=sfin_t, in_=S[:], reads=[b_S], dma=True)
            p.op("sp", "dma_start", out=S[:], in_=sin_h, writes=[b_S], dma=True)
            p.op("dve", "tensor_scalar_mul", S[:], S[:], smk[:, 0:1],
                 reads=[b_S, b_smk], writes=[b_S])

        NTT_B = NT // TB
        npb = TB // 128
        with contextlib.ExitStack() as _esB:
            wosb = _esB.enter_context(nc.sbuf_tensor("wosb", [128, ZB * D], BF16))
            cohE = _esB.enter_context(nc.sbuf_tensor("c_ohE", [H, 16 * 128], F32))
            xstm2 = _esB.enter_context(nc.sbuf_tensor("xstm2", [128, npb * DI], BF16))
            xsw2 = _esB.enter_context(nc.sbuf_tensor("xsw2", [128, npb * DI], BF16))
            silz2 = _esB.enter_context(nc.sbuf_tensor("silz2", [128, ZB * TB], BF16))
            cf2 = _esB.enter_context(nc.sbuf_tensor("cf2", [128, TB], BF16))
            bf2 = _esB.enter_context(nc.sbuf_tensor("bf2", [128, TB], BF16))
            bt2 = _esB.enter_context(nc.sbuf_tensor("bt2", [128, npb * DS], BF16))
            xt2 = _esB.enter_context(nc.sbuf_tensor("xt2", [128, npb * D], F32))
            erep = _esB.enter_context(nc.sbuf_tensor("erep", [128, ZB * TB], BF16))
            et2 = _esB.enter_context(nc.sbuf_tensor("et2", [H, TB + 64], F32))
            qtm = _esB.enter_context(nc.sbuf_tensor("qtm", [128, npb * H], F32))
            grw = _esB.enter_context(nc.sbuf_tensor("grw", [2, DI], F32))
            Ld = _esB.enter_context(nc.sbuf_tensor("Ld", [128, DI], BF16))
            M3 = _esB.enter_context(nc.sbuf_tensor("M3", [128, DI], BF16))
            gm2 = _esB.enter_context(nc.sbuf_tensor("gm2", [128, 2 * CH], BF16))
            uscr = _esB.enter_context(nc.sbuf_tensor("uscr", [128, DI], F32))
            yscr = _esB.enter_context(nc.sbuf_tensor("yscr", [128, 1024], F32))
            yg2 = _esB.enter_context(nc.sbuf_tensor("yg2", [128, 1024], BF16))
            yn2 = _esB.enter_context(nc.sbuf_tensor("yn2", [128, ZB * TB], BF16))
            qfm = _esB.enter_context(nc.sbuf_tensor("qfm", [H, 128], F32))
            b_wo, b_xstm2, b_xsw2 = Buf("wo"), Buf("xstm2"), Buf("xsw2")
            b_cohE = Buf("cohE")
            ld(cohE, b_cohE, ohE_h)
            b_silz2, b_cf2, b_bf2, b_bt2 = Buf("silz2"), Buf("cf2"), Buf("bf2"), Buf("bt2")
            b_xt2, b_erep, b_et2 = Buf("xt2"), Buf("erep"), Buf("et2")
            b_qtm, b_grw, b_Ld, b_M3 = Buf("qtm"), Buf("grw"), Buf("Ld"), Buf("M3")
            b_gm2, b_uscr, b_yscr = Buf("gm2"), Buf("uscr"), Buf("yscr")
            b_yg2, b_yn2, b_qfm = Buf("yg2"), Buf("yn2"), Buf("qfm")

            for kb in range(ZB):
                p.raw("gps", lambda g, kb=kb: g.dma_start(
                    out=wosb[:, kb * D:(kb + 1) * D],
                    in_=w_out_h[kb * 128:(kb + 1) * 128, :]),
                    writes=[b_wo], dma=True)
            for kb in range(ZB):
                p.op("dve", "tensor_scalar_mul",
                     wosb[:, kb * D:(kb + 1) * D], wosb[:, kb * D:(kb + 1) * D],
                     normw[:, kb:kb + 1],
                     reads=[b_wo, b_normw], writes=[b_wo])

            for it in range(NTT_B):
                t0 = it * TB
                for j in range(npb):
                    gt = t0 + j * 128
                    p.raw("sp", lambda s, j=j, gt=gt: s.dma_start(
                        out=xstm2[:, j * DI:(j + 1) * DI], in_=sp_xs[gt:gt + 128, :]),
                        reads=[bS["sp_xs"]], writes=[b_xstm2], dma=True)
                    p.raw("sp", lambda s, j=j, gt=gt: s.dma_start(
                        out=xsw2[:, j * DI:(j + 1) * DI], in_=sp_xsw[gt:gt + 128, :]),
                        reads=[bS["sp_xsw"]], writes=[b_xsw2], dma=True)
                    p.raw("sp", lambda s, j=j, gt=gt: s.dma_start(
                        out=bt2[:, j * DS:(j + 1) * DS], in_=sp_btm[gt:gt + 128, :]),
                        reads=[bS["sp_btm"]], writes=[b_bt2], dma=True)
                    p.raw("act", lambda s, j=j, gt=gt: s.dma_start(
                        out=xt2[:, j * D:(j + 1) * D], in_=x_in[gt:gt + 128, :]),
                        writes=[b_xt2], dma=True)
                p.raw("act", lambda s, t0=t0: s.dma_start(
                    out=AP(silz2, 0, [[ZB * TB, 128], [TB, ZB], [1, TB]]),
                    in_=AP(sp_z.tensor, t0, [[NT, 128], [128 * NT, ZB], [1, TB]])),
                    reads=[bS["sp_z"]], writes=[b_silz2], dma=True)
                p.raw("act", lambda s, t0=t0: s.dma_start(
                    out=cf2[:], in_=sp_cfm[:, t0:t0 + TB]),
                    reads=[bS["sp_cfm"]], writes=[b_cf2], dma=True)
                p.raw("act", lambda s, t0=t0: s.dma_start(
                    out=bf2[:], in_=sp_bfm[:, t0:t0 + TB]),
                    reads=[bS["sp_bfm"]], writes=[b_bf2], dma=True)
                nct = TB // CH
                c0 = t0 // CH
                gs = et2[:, TB:TB + nct]
                if t0 == 0:
                    p.op("dve", "memset", gs[:, 0:1], 0.0, writes=[b_et2])
                else:
                    p.raw("dve", lambda v, t0=t0: v.tensor_copy(
                        out=gs[:, 0:1], in_=gfm[:, t0 - 1:t0]),
                        reads=[b_gfm], writes=[b_et2])
                if nct > 1:
                    p.raw("dve", lambda v, t0=t0, nct=nct: v.tensor_copy(
                        out=gs[:, 1:nct],
                        in_=AP(gfm, t0 + CH - 1, [[NT, H], [CH, nct - 1]])),
                        reads=[b_gfm], writes=[b_et2])
                p.raw("dve", lambda v, t0=t0, nct=nct: v.tensor_tensor(
                    out=et2[:, 0:TB], in0=gfm[:, t0:t0 + TB],
                    in1=AP(et2, TB, [[TB + 64, H], [1, nct], [0, CH]]),
                    op=ALU.subtract),
                    reads=[b_gfm, b_et2], writes=[b_et2])
                p.op("act", "activation", et2[:, 0:TB], et2[:, 0:TB], AF.Exp,
                     reads=[b_et2], writes=[b_et2])
                for b in range(ZB):
                    p.op("te", "matmul", ps2[:, 0:TB],
                         cohE[:, b * 128:(b + 1) * 128], et2[:, 0:TB],
                         start=True, stop=True,
                         reads=[b_cohE, b_et2], writes=[b_ps2])
                    p.op("dve", "tensor_copy", out=erep[:, b * TB:(b + 1) * TB],
                         in_=ps2[:, 0:TB], reads=[b_ps2], writes=[b_erep])
                p.raw("act", lambda s, t0=t0: s.dma_start(
                    out=sp_g2[:, 0:TB], in_=gfm[:, t0:t0 + TB]),
                    reads=[b_gfm], writes=[bS["sp_g2"]], dma=True)


                for j in range(npb):
                    tp0 = t0 + j * 128
                    for cc_ in range(2):
                        p.raw("sp", lambda s, cc_=cc_, j=j: s.dma_start(
                            out=AP(grw, cc_ * DI, [[DI, 1], [CH, H], [1, CH]]),
                            in_=AP(sp_g2.tensor, j * 128 + cc_ * CH,
                                   [[512, H], [1, CH]])),
                            reads=[bS["sp_g2"]], writes=[b_grw], dma=True)
                    p.raw("dve", lambda v, tp0=tp0: v.tensor_sub(
                        out=qfm[:], in0=lndt[:, tp0:tp0 + 128],
                        in1=gfm[:, tp0:tp0 + 128]),
                        reads=[b_lndt, b_gfm], writes=[b_qfm])
                    p.op("te", "transpose", ps2[0:128, 896:896 + H], qfm[:],
                         cidf[0:H, 0:H], reads=[b_qfm, b_cidf], writes=[b_ps2])
                    p.op("dve", "tensor_copy", out=qtm[:, j * H:(j + 1) * H],
                         in_=ps2[0:128, 896:896 + H], reads=[b_ps2], writes=[b_qtm])
                    for q in range(4):
                        pstt = ps0 if q < 2 else ps1
                        b_pstt = b_ps0 if q < 2 else b_ps1
                        p.op("te", "matmul",
                             pstt[:, (q % 2) * 512:(q % 2 + 1) * 512],
                             ctb2[:], grw[:, q * 512:(q + 1) * 512],
                             start=True, stop=True,
                             reads=[b_ctb2, b_grw], writes=[b_pstt])
                    for half, (pp, bpp) in enumerate([(ps0, b_ps0), (ps1, b_ps1)]):
                        p.raw("dve", lambda v, half=half, pp=pp, j=j: v.tensor_tensor(
                            out=uscr[:, half * 1024:(half + 1) * 1024],
                            in0=pp[:, 0:1024].rearrange("p (h l) -> p h l", l=CH),
                            in1=AP(qtm, j * H + half * 16,
                                   [[npb * H, 128], [1, 16], [0, CH]]),
                            op=ALU.add),
                            reads=[bpp, b_qtm], writes=[b_uscr])
                    p.op("dve", "tensor_scalar_min", uscr[:], uscr[:], CAP,
                         reads=[b_uscr], writes=[b_uscr])
                    p.op("act", "activation", Ld[:], uscr[:], AF.Exp,
                         reads=[b_uscr], writes=[b_Ld])
                    for cc in range(2):
                        p.op("te", "matmul", ps2[cc * 64:(cc + 1) * 64, 0:CH],
                             bf2[:, tp0 - t0 + cc * 64:tp0 - t0 + (cc + 1) * 64],
                             cf2[:, tp0 - t0 + cc * 64:tp0 - t0 + (cc + 1) * 64],
                             start=True, stop=True, tile_position=(0, cc * 64),
                             skip_group_check=(cc == 1),
                             reads=[b_bf2, b_cf2], writes=[b_ps2])
                    p.op("dve", "tensor_tensor", out=gm2[:, 0:CH],
                         in0=ps2[:, 0:CH], in1=ctril[:], op=ALU.mult,
                         reads=[b_ps2, b_ctril], writes=[b_gm2])
                    p.raw("dve", lambda v: v.tensor_tensor(
                        out=M3[:].rearrange("p (h l) -> p h l", l=CH),
                        in0=Ld[:].rearrange("p (h l) -> p h l", l=CH),
                        in1=AP(gm2, 0, [[2 * CH, 128], [0, H], [1, CH]]),
                        op=ALU.mult),
                        reads=[b_Ld, b_gm2], writes=[b_M3])
                    p.op("dve", "tensor_add", out=M3[:], in0=M3[:], in1=ddiag[:],
                         reads=[b_M3, b_ddiag], writes=[b_M3])

                    for cc in range(2):
                        c_glob = tp0 // CH + cc
                        loff = (j * 128 + cc * 64)
                        p.op("act", "activation", Sb_[:], S[:], AF.Identity,
                             reads=[b_S], writes=[b_Sb])
                        for b in range(ZB):
                            p.op("te", "matmul", ps1[:, b * 64:(b + 1) * 64],
                                 Sb_[:, b * 128:(b + 1) * 128],
                                 cf2[:, loff:loff + CH],
                                 start=True, stop=True, skip_group_check=(b > 0),
                                 reads=[b_Sb, b_cf2], writes=[b_ps1])
                        for b in range(ZB):
                            for hh in range(2):
                                p.op("te", "matmul",
                                     ps0[hh * 64:(hh + 1) * 64, b * 64:(b + 1) * 64],
                                     xstm2[cc * 64:(cc + 1) * 64,
                                           j * DI + b * 128 + hh * 64:
                                           j * DI + b * 128 + (hh + 1) * 64],
                                     M3[cc * 64:(cc + 1) * 64,
                                        (2 * b + hh) * CH:(2 * b + hh + 1) * CH],
                                     start=True, stop=True,
                                     tile_position=(cc * 64, hh * 64),
                                     skip_group_check=(b + hh > 0),
                                     reads=[b_xstm2, b_M3], writes=[b_ps0])
                        p.raw("dve", lambda v, loff=loff: v.tensor_tensor(
                            out=yscr[:].rearrange("p (b l) -> p b l", l=CH),
                            in0=ps1[:, 0:1024].rearrange("p (b l) -> p b l", l=CH),
                            in1=AP(erep, loff, [[ZB * TB, 128], [TB, ZB], [1, CH]]),
                            op=ALU.mult),
                            reads=[b_ps1, b_erep], writes=[b_yscr])
                        p.op("dve", "tensor_add", out=yscr[:], in0=yscr[:],
                             in1=ps0[:, 0:1024], reads=[b_yscr, b_ps0],
                             writes=[b_yscr])
                        p.raw("dve", lambda v, loff=loff: v.tensor_tensor(
                            out=yscr[:].rearrange("p (b l) -> p b l", l=CH),
                            in0=yscr[:].rearrange("p (b l) -> p b l", l=CH),
                            in1=AP(silz2, loff, [[ZB * TB, 128], [TB, ZB], [1, CH]]),
                            op=ALU.mult),
                            reads=[b_yscr, b_silz2], writes=[b_yscr])
                        p.op("act", "activation", yg2[:], yscr[:], AF.Square,
                             reads=[b_yscr], writes=[b_yg2])
                        for b in range(ZB):
                            p.op("te", "matmul", ps2[0:1, 512:512 + CH],
                                 coK[:], yg2[:, b * 64:(b + 1) * 64],
                                 start=(b == 0), stop=(b == ZB - 1),
                                 reads=[b_coK, b_yg2], writes=[b_ps2])
                        p.op("dve", "tensor_scalar", out=rowa[0:1, 0:CH],
                             in0=ps2[0:1, 512:512 + CH], scalar1=1.0 / DI,
                             scalar2=EPS, op0=ALU.mult, op1=ALU.add,
                             reads=[b_ps2], writes=[b_rowa])
                        p.op("act", "activation", rowa[0:1, 0:CH],
                             rowa[0:1, 0:CH], AF.Ln,
                             reads=[b_rowa], writes=[b_rowa])
                        p.op("act", "activation", rowa[0:1, 0:CH],
                             rowa[0:1, 0:CH], AF.Exp, scale=-0.5,
                             reads=[b_rowa], writes=[b_rowa])
                        p.op("te", "matmul", ps2[:, 768:768 + CH], co1f[:],
                             rowa[0:1, 0:CH], start=True, stop=True,
                             reads=[b_co1f, b_rowa], writes=[b_ps2])
                        p.raw("dve", lambda v, loff=loff: v.scalar_tensor_tensor(
                            out=AP(yn2, loff, [[ZB * TB, 128], [TB, ZB], [1, CH]]),
                            in0=yscr[:].rearrange("p (b l) -> p b l", l=CH),
                            scalar=1.0,
                            in1=AP(ps2, 768, [[1024, 128], [0, ZB], [1, CH]]),
                            op0=ALU.mult, op1=ALU.mult),
                            reads=[b_yscr, b_ps2], writes=[b_yn2])
                        for q in range(4):
                            pstt = ps0 if q < 2 else ps1
                            b_pstt = b_ps0 if q < 2 else b_ps1
                            p.op("te", "matmul",
                                 pstt[:, (q % 2) * 512:(q % 2 + 1) * 512],
                                 bt2[cc * 64:(cc + 1) * 64, j * DS:(j + 1) * DS],
                                 xsw2[cc * 64:(cc + 1) * 64,
                                      j * DI + q * 512:j * DI + (q + 1) * 512],
                                 start=True, stop=True, tile_position=(cc * 64, 0),
                                 reads=[b_bt2, b_xsw2], writes=[b_pstt])
                        p.raw("te", lambda t, c_glob=c_glob: t.matmul(
                            ps2[:, 0:H], co1f[:],
                            dcrow[0:1, c_glob * H:(c_glob + 1) * H],
                            start=True, stop=True),
                            reads=[b_co1f, b_dcrow], writes=[b_ps2])
                        p.raw("dve", lambda v: v.tensor_tensor(
                            out=St[:], in0=S[:],
                            in1=AP(ps2, 0, [[1024, 128], [1, H], [0, PH]]),
                            op=ALU.mult),
                            reads=[b_S, b_ps2], writes=[b_St])
                        p.raw("dve", lambda v: v.tensor_add(
                            out=S[:, 0:1024], in0=St[:, 0:1024], in1=ps0[:, :]),
                            reads=[b_St, b_ps0], writes=[b_S])
                        p.raw("dve", lambda v: v.tensor_add(
                            out=S[:, 1024:2048], in0=St[:, 1024:2048], in1=ps1[:, :]),
                            reads=[b_St, b_ps1], writes=[b_S])

                for j in range(npb):
                    for kb in range(ZB):
                        for hf in range(2):
                            p.op("te", "matmul",
                                 ps2[:, hf * 512:(hf + 1) * 512],
                                 yn2[:, kb * TB + j * 128:kb * TB + (j + 1) * 128],
                                 wosb[:, kb * D + hf * 512:kb * D + (hf + 1) * 512],
                                 start=(kb == 0), stop=(kb == ZB - 1),
                                 reads=[b_yn2, b_wo], writes=[b_ps2])
                    sums = stat[:, 0:1]
                    ssq = stat[:, 1:2]
                    mu = stat[:, 2:3]
                    ivar = stat[:, 3:4]
                    scr = stat[:, 4:5]
                    p.op("dve", "reduce_sum", sums, ps2[:, 0:D],
                         axis=mybir.AxisListType.X, reads=[b_ps2], writes=[b_stat])
                    p.op("act", "activation", uscr[:, 0:D], ps2[:, 0:D],
                         AF.Square, accum_out=ssq,
                         reads=[b_ps2], writes=[b_uscr, b_stat])
                    p.op("dve", "tensor_scalar_mul", mu, sums, 1.0 / D,
                         reads=[b_stat], writes=[b_stat])
                    p.op("dve", "tensor_mul", out=scr, in0=mu, in1=mu,
                         reads=[b_stat], writes=[b_stat])
                    p.op("dve", "tensor_scalar_mul", ivar, ssq, 1.0 / D,
                         reads=[b_stat], writes=[b_stat])
                    p.op("dve", "tensor_sub", out=ivar, in0=ivar, in1=scr,
                         reads=[b_stat], writes=[b_stat])
                    p.op("dve", "tensor_scalar_add", ivar, ivar, EPS,
                         reads=[b_stat], writes=[b_stat])
                    p.op("act", "activation", ivar, ivar, AF.Ln,
                         reads=[b_stat], writes=[b_stat])
                    p.op("act", "activation", ivar, ivar, AF.Exp, scale=-0.5,
                         reads=[b_stat], writes=[b_stat])
                    p.op("dve", "tensor_scalar", out=uscr[:, 0:D],
                         in0=ps2[:, 0:D], scalar1=mu, scalar2=ivar,
                         op0=ALU.subtract, op1=ALU.mult,
                         reads=[b_ps2, b_stat], writes=[b_uscr])
                    p.op("dve", "tensor_add", out=xt2[:, j * D:(j + 1) * D],
                         in0=xt2[:, j * D:(j + 1) * D], in1=uscr[:, 0:D],
                         reads=[b_uscr, b_xt2], writes=[b_xt2])
                    gt = t0 + j * 128
                    p.raw("sp", lambda s, j=j, gt=gt: s.dma_start(
                        out=sp_x1[gt:gt + 128, :], in_=xt2[:, j * D:(j + 1) * D]),
                        reads=[b_xt2], writes=[bS["sp_x1"]], dma=True)

        p.barrier()
        _esM.close()
        NTT_C = NT // TC
        npc = TC // 128
        with contextlib.ExitStack() as _esC:
            _esF = contextlib.ExitStack()
            cxp = _esC.enter_context(nc.sbuf_tensor("c_xp", [128, 8 * 128], BF16))
            bfv = _esC.enter_context(nc.sbuf_tensor("bfv_sb", [128, 8 * NST * 32], BF16))
            wex = _esC.enter_context(nc.sbuf_tensor("wex", [128, 8 * HD], BF16))
            qcn = _esC.enter_context(nc.sbuf_tensor("qcn", [128, 32 * D], BF16))
            bpre = _esC.enter_context(nc.sbuf_tensor("bpre", [128, 32], F32))
            ubp = _esC.enter_context(nc.sbuf_tensor("ubp", [128, 32], F32))
            x1t = _esC.enter_context(nc.sbuf_tensor("x1t", [128, npc * D], F32))
            x1n = _esC.enter_context(nc.sbuf_tensor("x1n", [128, npc * D], BF16))
            x1f = _esC.enter_context(nc.sbuf_tensor("x1f", [128, 8 * TC], BF16))
            gp = _esC.enter_context(nc.sbuf_tensor("gp", [128, 32 * TC], BF16))
            fscr = _esC.enter_context(nc.sbuf_tensor("fscr", [128, D], F32))
            stg = _esF.enter_context(nc.sbuf_tensor("stg", [128, HD], BF16))
            chn = _esF.enter_context(nc.sbuf_tensor("chn", [128, HD], BF16))
            wblk = _esF.enter_context(nc.sbuf_tensor("wblk", [128, D], BF16))
            wstg = _esF.enter_context(nc.sbuf_tensor("wstg", [128, D], BF16))
            bvec = _esF.enter_context(nc.sbuf_tensor("bvec", [128, 32], BF16))
            vstg = _esF.enter_context(nc.sbuf_tensor("vstg", [128, 32], F32))
            b_bfv, b_wex, b_qcn, b_bpre = Buf("bfv"), Buf("wex"), Buf("qcn"), Buf("bpre")
            b_cxp = Buf("cxp")
            ld(cxp, b_cxp, xpat_h)
            b_stg, b_sscr, b_r0, b_r1 = Buf("stg"), Buf("sscr"), Buf("r0"), Buf("r1")
            b_chn = Buf("chn")
            b_wblk, b_wstg, b_bvec, b_vstg, b_ubp = (
                Buf("wblk"), Buf("wstg"), Buf("bvec"), Buf("vstg"), Buf("ubp"))
            b_x1t, b_x1n, b_x1f, b_gp = Buf("x1t"), Buf("x1n"), Buf("x1f"), Buf("gp")
            assert 32 * TC >= 2 * HD, "gp too small for fold scratch"
            sscr, b_sscr = gp, b_gp
            b_fscr = Buf("fscr")

            p.op("gps", "dma_start", out=bfv[:], in_=bfv_h, writes=[b_bfv], dma=True)
            p.op("gps", "dma_start", out=ubp[:], in_=up_bias_p_h, writes=[b_ubp], dma=True)

            def build_stage(dst, b_dst, vset, s):
                ai, bi = vset
                st = (1 << s) if s < 7 else (1 << (s - 7))
                sti = int(np.log2(st))
                aoff = (ai * NST + s) * 32
                boff = (bi * NST + s) * 32
                p.raw("dve", lambda v: v.tensor_tensor(
                    out=dst[:].rearrange("p (b c) -> p b c", c=128),
                    in0=AP(cxp, 7 * 128, [[8 * 128, 128], [0, 32], [1, 128]]),
                    in1=AP(bfv, aoff, [[8 * NST * 32, 128], [1, 32], [0, 128]]),
                    op=ALU.mult),
                    reads=[b_cxp, b_bfv], writes=[b_dst])
                p.raw("dve", lambda v: v.tensor_tensor(
                    out=sscr[:, 0:HD].rearrange("p (b c) -> p b c", c=128),
                    in0=AP(cxp, sti * 128, [[8 * 128, 128], [0, 32], [1, 128]]),
                    in1=AP(bfv, boff, [[8 * NST * 32, 128], [1, 32], [0, 128]]),
                    op=ALU.mult),
                    reads=[b_cxp, b_bfv], writes=[b_sscr])
                p.op("dve", "tensor_add", out=dst[:], in0=dst[:], in1=sscr[:, 0:HD],
                     reads=[b_dst, b_sscr], writes=[b_dst])

            def chain(dst, b_dst, seed_vset, seed_s, steps):
                r0v = gp[:, HD:2 * HD]
                bufs = ([(r0v, b_gp), (dst, b_dst)] if len(steps) % 2 == 1
                        else [(dst, b_dst), (r0v, b_gp)])
                build_stage(bufs[0][0], bufs[0][1], seed_vset, seed_s)
                cur, b_cur = bufs[0]
                nxt, b_nxt = bufs[1]
                for vset, s in steps:
                    build_stage(stg, b_stg, vset, s)
                    for b in range(32):
                        p.op("te", "matmul", ps0[:, 0:128],
                             stg[:, b * 128:(b + 1) * 128],
                             cur[:, b * 128:(b + 1) * 128],
                             start=True, stop=True, skip_group_check=True,
                             reads=[b_stg, b_cur], writes=[b_ps0])
                        p.op("dve", "tensor_copy",
                             out=nxt[:, b * 128:(b + 1) * 128], in_=ps0[:, 0:128],
                             reads=[b_ps0], writes=[b_nxt])
                    (cur, b_cur), (nxt, b_nxt) = (nxt, b_nxt), (cur, b_cur)

            def dump_chain(tag):
                if "fold" not in dbgset:
                    return
                dd = nc.dram_tensor("dbgc_" + tag, [128, HD], BF16,
                                    kind="ExternalOutput").ap()
                p.op("sp", "dma_start", out=dd, in_=chn[:], reads=[b_chn], dma=True)

            def dump_sb(tag, t, b, shape, dtp):
                if "fold" not in dbgset:
                    return
                dd = nc.dram_tensor("dbgs_" + tag, shape, dtp,
                                    kind="ExternalOutput").ap()
                p.op("sp", "dma_start", out=dd, in_=t[:], reads=[b], dma=True)

            dump_sb("cxp", cxp, b_cxp, [128, 8 * 128], BF16)
            dump_sb("bfv", bfv, b_bfv, [128, 8 * NST * 32], BF16)

            chain(chn, b_chn, (2, 3), 6, [((0, 1), k) for k in [5, 4, 3, 2, 1, 0]])
            dump_chain("bd1uT")
            for blk in range(32):
                p.raw("gps", lambda g, blk=blk: g.dma_start(
                    out=wblk[:, 0:D],
                    in_=w_expT_h[blk * 128:(blk + 1) * 128, :]),
                    writes=[b_wblk], dma=True)
                for hf in range(2):
                    p.op("te", "matmul", ps0[:, hf * 512:(hf + 1) * 512],
                         chn[:, blk * 128:(blk + 1) * 128],
                         wblk[:, hf * 512:(hf + 1) * 512],
                         start=True, stop=True, skip_group_check=True,
                         reads=[b_chn, b_wblk], writes=[b_ps0])
                p.op("dve", "tensor_copy", out=wstg[:], in_=ps0[:, 0:D],
                     reads=[b_ps0], writes=[b_wstg])
                p.raw("sp", lambda s, blk=blk: s.dma_start(
                    out=sp_w[blk * 128:(blk + 1) * 128, :], in_=wstg[:]),
                    reads=[b_wstg], writes=[bS["sp_w"]], dma=True)
            p.op("gps", "dma_start", out=vstg[:], in_=b_exp2_h,
                 writes=[b_vstg], dma=True)
            p.op("dve", "tensor_copy", out=bvec[:], in_=vstg[:],
                 reads=[b_vstg], writes=[b_bvec])
            for blk in range(32):
                p.op("te", "matmul", ps1[:, blk:blk + 1],
                     chn[:, blk * 128:(blk + 1) * 128], bvec[:, blk:blk + 1],
                     start=True, stop=True, skip_group_check=True,
                     reads=[b_chn, b_bvec], writes=[b_ps1])
            p.op("dve", "tensor_copy", out=vstg[:], in_=ps1[:, 0:32],
                 reads=[b_ps1], writes=[b_vstg])
            p.raw("sp", lambda s: s.dma_start(out=sp_bv, in_=vstg[:]),
                  reads=[b_vstg], writes=[bS["sp_bv"]], dma=True)

            chain(chn, b_chn, (2, 3), 11, [((0, 1), k) for k in [10, 9, 8, 7]])
            dump_chain("c2uT")
            for bp in range(32):
                for jq in range(4):
                    p.raw("sp", lambda s, bp=bp, jq=jq: s.dma_start(
                        out=wblk[jq * 32:(jq + 1) * 32, 0:D],
                        in_=AP(sp_w.tensor, (4 * bp + jq) * D,
                               [[128 * D, 32], [1, D]])),
                        reads=[bS["sp_w"]], writes=[b_wblk], dma=True)
                for hf in range(2):
                    p.op("te", "matmul", ps0[:, hf * 512:(hf + 1) * 512],
                         chn[:, bp * 128:(bp + 1) * 128],
                         wblk[:, hf * 512:(hf + 1) * 512],
                         start=True, stop=True, skip_group_check=True,
                         reads=[b_chn, b_wblk], writes=[b_ps0])
                p.op("dve", "tensor_copy", out=wstg[:], in_=ps0[:, 0:D],
                     reads=[b_ps0], writes=[b_wstg])
                p.raw("sp", lambda s, bp=bp: s.dma_start(
                    out=sp_w2[bp * 128:(bp + 1) * 128, :], in_=wstg[:]),
                    reads=[b_wstg], writes=[bS["sp_w2"]], dma=True)
            for k in range(8):
                p.raw("sp", lambda s, k=k: s.dma_start(
                    out=wex[:, k * HD:(k + 1) * HD],
                    in_=sp_w2[:, k * 128:(k + 1) * 128], transpose=True),
                    reads=[bS["sp_w2"]], writes=[b_wex], dma=True)
            for jq in range(4):
                p.raw("sp", lambda s, jq=jq: s.dma_start(
                    out=vstg[jq * 32:(jq + 1) * 32, 0:32],
                    in_=AP(sp_bv.tensor, jq, [[128, 32], [4, 32]]),
                    allow_slow_non_contiguous=True),
                    reads=[bS["sp_bv"]], writes=[b_vstg], dma=True)
            p.op("dve", "tensor_copy", out=bvec[:], in_=vstg[:],
                 reads=[b_vstg], writes=[b_bvec])
            for blk in range(32):
                p.op("te", "matmul", ps1[:, blk:blk + 1],
                     chn[:, blk * 128:(blk + 1) * 128], bvec[:, blk:blk + 1],
                     start=True, stop=True, skip_group_check=True,
                     reads=[b_chn, b_bvec], writes=[b_ps1])
            p.op("dve", "tensor_add", out=bpre[:], in0=ps1[:, 0:32], in1=ubp[:],
                 reads=[b_ps1, b_ubp], writes=[b_bpre])

            chain(chn, b_chn, (4, 5), 7, [((6, 7), k) for k in [8, 9, 10, 11]])
            dump_chain("c2d")
            for blk in range(32):
                p.raw("gps", lambda g, blk=blk: g.dma_start(
                    out=wblk[:, 0:D],
                    in_=w_conP_h[blk * 128:(blk + 1) * 128, :]),
                    writes=[b_wblk], dma=True)
                for hf in range(2):
                    p.op("te", "matmul", ps0[:, hf * 512:(hf + 1) * 512],
                         chn[:, blk * 128:(blk + 1) * 128],
                         wblk[:, hf * 512:(hf + 1) * 512],
                         start=True, stop=True, skip_group_check=True,
                         reads=[b_chn, b_wblk], writes=[b_ps0])
                p.op("dve", "tensor_copy", out=wstg[:], in_=ps0[:, 0:D],
                     reads=[b_ps0], writes=[b_wstg])
                p.raw("sp", lambda s, blk=blk: s.dma_start(
                    out=sp_w[blk * 128:(blk + 1) * 128, :], in_=wstg[:]),
                    reads=[b_wstg], writes=[bS["sp_w"]], dma=True)

            chain(chn, b_chn, (4, 5), 0, [((6, 7), k) for k in [1, 2, 3, 4, 5, 6]])
            dump_chain("bd1d")
            for blk in range(32):
                p.raw("sp", lambda s, blk=blk: s.dma_start(
                    out=wblk[:, 0:D],
                    in_=AP(sp_w.tensor, blk * D, [[32 * D, 128], [1, D]])),
                    reads=[bS["sp_w"]], writes=[b_wblk], dma=True)
                for hf in range(2):
                    p.op("te", "matmul", ps0[:, hf * 512:(hf + 1) * 512],
                         chn[:, blk * 128:(blk + 1) * 128],
                         wblk[:, hf * 512:(hf + 1) * 512],
                         start=True, stop=True, skip_group_check=True,
                         reads=[b_chn, b_wblk], writes=[b_ps0])
                p.op("dve", "tensor_copy", out=wstg[:], in_=ps0[:, 0:D],
                     reads=[b_ps0], writes=[b_wstg])
                p.raw("sp", lambda s, blk=blk: s.dma_start(
                    out=sp_w2[blk * 128:(blk + 1) * 128, :], in_=wstg[:]),
                    reads=[b_wstg], writes=[bS["sp_w2"]], dma=True)
            for bp in range(32):
                for jq in range(4):
                    p.raw("sp", lambda s, bp=bp, jq=jq: s.dma_start(
                        out=qcn[jq * 32:(jq + 1) * 32, bp * D:(bp + 1) * D],
                        in_=AP(sp_w2.tensor, (4 * bp + jq) * D,
                               [[128 * D, 32], [1, D]])),
                        reads=[bS["sp_w2"]], writes=[b_qcn], dma=True)

            _esF.close()
            for it in range(NTT_C):
                t0 = it * TC
                for j in range(npc):
                    gt = t0 + j * 128
                    p.raw("sp", lambda s, j=j, gt=gt: s.dma_start(
                        out=x1t[:, j * D:(j + 1) * D], in_=sp_x1[gt:gt + 128, :]),
                        reads=[bS["sp_x1"]], writes=[b_x1t], dma=True)
                for j in range(npc):
                    ln_tm(x1t, b_x1t, x1n, b_x1n, 128, D, j * D)
                for j in range(npc):
                    for k in range(8):
                        pt, b_pt = PT[(j * 8 + k) % 2]
                        p.op("te", "transpose", pt[:, 0:128],
                             x1n[:, j * D + k * 128:j * D + (k + 1) * 128],
                             cid[:], reads=[b_x1n, b_cid], writes=[b_pt])
                        p.op("dve", "tensor_copy",
                             out=x1f[:, k * TC + j * 128:k * TC + (j + 1) * 128],
                             in_=pt[:, 0:128], reads=[b_pt], writes=[b_x1f])
                slots = [0, 512]
                si = 0
                for fb in range(32):
                    off = slots[si]; si ^= 1
                    for k in range(8):
                        p.op("te", "matmul", ps0[:, off:off + TC],
                             wex[:, k * HD + fb * 128:k * HD + (fb + 1) * 128],
                             x1f[:, k * TC:(k + 1) * TC],
                             start=(k == 0), stop=(k == 7), skip_group_check=True,
                             reads=[b_wex, b_x1f], writes=[b_ps0])
                    p.op("act", "activation", gp[:, fb * TC:(fb + 1) * TC],
                         ps0[:, off:off + TC], AF.Gelu, bias=bpre[:, fb:fb + 1],
                         reads=[b_ps0, b_bpre], writes=[b_gp])
                for j in range(npc):
                    for fb in range(32):
                        for hf in range(2):
                            p.op("te", "matmul", ps1[:, hf * 512:(hf + 1) * 512],
                                 gp[:, fb * TC + j * 128:fb * TC + (j + 1) * 128],
                                 qcn[:, fb * D + hf * 512:fb * D + (hf + 1) * 512],
                                 start=(fb == 0), stop=False, skip_group_check=True,
                                 reads=[b_gp, b_qcn], writes=[b_ps1])
                    for hf in range(2):
                        p.op("te", "matmul", ps1[:, hf * 512:(hf + 1) * 512],
                             co1[:], boutsb[0:1, hf * 512:(hf + 1) * 512],
                             start=False, stop=True, skip_group_check=True,
                             reads=[b_co1, b_bout], writes=[b_ps1])
                    sums = stat[:, 0:1]
                    ssq = stat[:, 1:2]
                    mu = stat[:, 2:3]
                    ivar = stat[:, 3:4]
                    scr = stat[:, 4:5]
                    p.op("dve", "reduce_sum", sums, ps1[:, 0:D],
                         axis=mybir.AxisListType.X, reads=[b_ps1], writes=[b_stat])
                    p.op("act", "activation", fscr[:], ps1[:, 0:D],
                         AF.Square, accum_out=ssq,
                         reads=[b_ps1], writes=[b_fscr, b_stat])
                    p.op("dve", "tensor_scalar_mul", mu, sums, 1.0 / D,
                         reads=[b_stat], writes=[b_stat])
                    p.op("dve", "tensor_mul", out=scr, in0=mu, in1=mu,
                         reads=[b_stat], writes=[b_stat])
                    p.op("dve", "tensor_scalar_mul", ivar, ssq, 1.0 / D,
                         reads=[b_stat], writes=[b_stat])
                    p.op("dve", "tensor_sub", out=ivar, in0=ivar, in1=scr,
                         reads=[b_stat], writes=[b_stat])
                    p.op("dve", "tensor_scalar_add", ivar, ivar, EPS,
                         reads=[b_stat], writes=[b_stat])
                    p.op("act", "activation", ivar, ivar, AF.Ln,
                         reads=[b_stat], writes=[b_stat])
                    p.op("act", "activation", ivar, ivar, AF.Exp, scale=-0.5,
                         reads=[b_stat], writes=[b_stat])
                    p.op("dve", "tensor_scalar", out=fscr[:],
                         in0=ps1[:, 0:D], scalar1=mu, scalar2=ivar,
                         op0=ALU.subtract, op1=ALU.mult,
                         reads=[b_ps1, b_stat], writes=[b_fscr])
                    p.op("dve", "tensor_add", out=fscr[:],
                         in0=fscr[:], in1=x1t[:, j * D:(j + 1) * D],
                         reads=[b_fscr, b_x1t], writes=[b_fscr])
                    gt = t0 + j * 128
                    p.raw("sp", lambda s, gt=gt: s.dma_start(
                        out=out_ap[gt:gt + 128, :], in_=fscr[:]),
                        reads=[b_fscr], dma=True)

        for dn in dbgset - {"fold"}:
            dd = nc.dram_tensor("dbg_" + dn, list(locals()[dn].tensor.shape),
                                locals()[dn].tensor.dtype,
                                kind="ExternalOutput").ap()
            p.op("sp", "dma_start", out=dd, in_=locals()[dn],
                 reads=[bS[dn]] if dn in bS else [], dma=True)
        p.fence("sp", [(s, p.cnt[s]) for s in p.cnt if s.startswith("D")])
    return nc


def _perm_f2fp():
    f = np.arange(HD)
    return (f % 128) * 32 + f // 128


def _bf_vectors(weights):
    P = _perm_f2fp()
    a = np.zeros((NST, HD), np.float32)
    b = np.zeros((NST, HD), np.float32)
    aT = np.zeros((NST, HD), np.float32)
    bT = np.zeros((NST, HD), np.float32)
    f = np.arange(HD)
    for s in range(NST):
        st = 1 << s
        nb = HD // (2 * st)
        w = weights[s].reshape(nb, st, 2, 2)
        blk = f // (2 * st)
        hi = (f // st) % 2
        r = f % st
        av = np.where(hi == 0, w[blk, r, 0, 0], w[blk, r, 1, 1]).astype(np.float32)
        bv = np.where(hi == 0, w[blk, r, 0, 1], w[blk, r, 1, 0]).astype(np.float32)
        if s < 7:
            a[s] = av
            b[s] = bv
            aT[s] = av
            bT[s] = bv[f ^ st]
        else:
            avp = np.empty(HD, np.float32)
            bvp = np.empty(HD, np.float32)
            avp[P[f]] = av
            bvp[P[f]] = bv
            a[s] = avp
            b[s] = bvp
            aT[s] = avp
            stp = 1 << (s - 7)
            bT[s] = bvp[np.arange(HD) ^ stp]
    return a, b, aT, bT


def _fm_blocks(vec):
    return np.ascontiguousarray(vec.reshape(-1, 128).T)


def _vec_layout(vs):
    out = np.zeros((128, NST * 32), np.float32)
    for s in range(NST):
        out[:, s * 32:(s + 1) * 32] = vs[s].reshape(32, 128).T
    return out


_NC_CACHE = {}


def kernel(L0_attn_x, L0_attn_W_in, L0_attn_conv_w, L0_attn_conv_b,
           L0_attn_dt_bias, L0_attn_A_log, L0_attn_D, L0_attn_norm_w,
           L0_attn_W_out, L0_ffn_W_exp, L0_ffn_b_exp, L0_ffn_up_weights,
           L0_ffn_up_bias, L0_ffn_down_weights, L0_ffn_W_con, L0_ffn_b_con,
           L0_ffn_down_bias):
    import ml_dtypes
    x = np.ascontiguousarray(np.asarray(L0_attn_x, dtype=np.float32))
    B, L, _ = x.shape
    NT = (B * L) // NC_
    half = L // 2

    key = ("fused", NT)
    if key not in _NC_CACHE:
        _NC_CACHE[key] = build_core_program(NT=NT, TA=256, TB=256, TC=256,
                                            n_cores=NC_, exchange="cc")
    nc = _NC_CACHE[key]

    f32 = np.float32
    bf16 = ml_dtypes.bfloat16
    w_in = np.ascontiguousarray(np.asarray(L0_attn_W_in, f32))
    conv_w = np.asarray(L0_attn_conv_w, f32).reshape(KC, 2304)
    conv_w_l = np.zeros((128, KC * XB), f32)
    for k in range(KC):
        conv_w_l[:, k * XB:(k + 1) * XB] = conv_w[k].reshape(XB, 128).T
    conv_b_l = np.asarray(L0_attn_conv_b, f32).reshape(XB, 128).T.copy()
    dt_bias = np.asarray(L0_attn_dt_bias, f32).reshape(H, 1)
    a_log = np.asarray(L0_attn_A_log, f32).reshape(H, 1)
    dvec = np.asarray(L0_attn_D, f32).reshape(1, H)
    norm_w_l = np.asarray(L0_attn_norm_w, f32).reshape(ZB, 128).T.copy()
    w_out = np.ascontiguousarray(np.asarray(L0_attn_W_out, f32))
    P = _perm_f2fp()
    w_expT = np.ascontiguousarray(np.asarray(L0_ffn_W_exp, f32).T)
    w_conP = np.zeros((HD, D), f32)
    w_conP[P] = np.asarray(L0_ffn_W_con, f32)
    b_exp2 = _fm_blocks(np.asarray(L0_ffn_b_exp, f32))
    up_bias_p = np.zeros(HD, f32)
    up_bias_p[P] = np.asarray(L0_ffn_up_bias, f32)
    up_bias_p = _fm_blocks(up_bias_p)
    b_out = np.asarray(L0_ffn_b_con, f32).reshape(1, D)
    down_b = np.asarray(L0_ffn_down_bias, f32).reshape(1, D)
    ua, ub, uaT, ubT = _bf_vectors(np.asarray(L0_ffn_up_weights, f32))
    da, db, daT, dbT = _bf_vectors(np.asarray(L0_ffn_down_weights, f32))
    bfv = np.concatenate([_vec_layout(v) for v in
                          (ua, ub, uaT, ubT, da, db, daT, dbT)],
                         axis=1).astype(bf16)
    ident = np.eye(128, dtype=bf16)
    xpat = np.zeros((128, 8 * 128), bf16)
    for sti in range(7):
        st = 1 << sti
        idx = np.arange(128)
        m = np.zeros((128, 128), f32)
        m[idx, idx ^ st] = 1.0
        xpat[:, sti * 128:(sti + 1) * 128] = m.astype(bf16)
    xpat[:, 7 * 128:8 * 128] = np.eye(128, dtype=bf16)
    tril = np.zeros((128, CH), f32)
    ss = np.arange(128) % 64
    tril[:, :] = (ss[:, None] <= np.arange(CH)[None, :]).astype(f32)
    tb2 = np.zeros((2, 128), f32)
    tb2[0, 0:64] = 1.0
    tb2[1, 64:128] = 1.0
    ohE = np.zeros((H, 16 * 128), f32)
    for b in range(16):
        ohE[2 * b, b * 128:b * 128 + 64] = 1.0
        ohE[2 * b + 1, b * 128 + 64:(b + 1) * 128] = 1.0

    shared = dict(w_in=w_in, conv_w=conv_w_l, conv_b=conv_b_l,
                  dt_bias=dt_bias, a_log=a_log, dvec=dvec, norm_w=norm_w_l,
                  w_out=w_out, w_expT=w_expT, w_conP=w_conP, b_exp2=b_exp2,
                  up_bias_p=up_bias_p, b_out=b_out, down_b=down_b, bfv=bfv,
                  ident=np.asarray(ident), xpat=np.asarray(xpat), tril=tril,
                  tb2=np.asarray(tb2), ohE=np.asarray(ohE))

    in_maps = []
    for c in range(NC_):
        seq = c // 2
        hi = c % 2
        xs = x[seq, hi * half:(hi + 1) * half, :]
        if hi == 0:
            xh = np.zeros((4, D), f32)
        else:
            xh = np.zeros((4, D), f32)
            xh[0:3] = x[seq, half - 3:half, :]
        smask = np.zeros((128, 8), f32)
        if hi == 1:
            smask[:, c - 1] = 1.0
        m = dict(shared)
        m["x"] = np.ascontiguousarray(xs)
        m["xh"] = xh
        m["smask"] = smask
        in_maps.append(m)

    res = run_bass_kernel_spmd(nc, in_maps, core_ids=list(range(NC_)))
    out = np.empty((B, L, D), np.float32)
    for c in range(NC_):
        seq = c // 2
        hi = c % 2
        out[seq, hi * half:(hi + 1) * half, :] = res.results[c]["out"]
    return out


# version 05
# speedup vs baseline: 1.0678x; 1.0678x over previous
import numpy as np
import concourse.bass as bass
import concourse.mybir as mybir
from concourse.bass_utils import run_bass_kernel_spmd
from concourse.ap import AP

F32 = mybir.dt.float32
BF16 = mybir.dt.bfloat16
AF = mybir.ActivationFunctionType
ALU = mybir.AluOpType

D = 1024
DI = 2048
DS = 128
H = 32
PH = 64
KC = 4
DTP = 4384
CH = 64
HD = 4096
NST = 12
EPS = 1e-5
CAP = 30.0
ZB = 16
XB = 18
NC_ = 8


class Buf:
    __slots__ = ("w", "r", "name")

    def __init__(self, name=""):
        self.w = None
        self.r = []
        self.name = name


class Prog:
    ENG = {"te": "tensor", "act": "scalar", "dve": "vector", "gps": "gpsimd",
           "sp": "sync"}
    DMAQ = ("sp", "gps", "act")
    NPOOL = 8

    def __init__(self, nc, block, sems):
        self.nc = nc
        self.block = block
        self.sems = sems
        self.cnt = {k: 0 for k in sems}
        self.waited = {e: {} for e in self.ENG}
        self.dmaslot = {q: 0 for q in self.DMAQ}

    def _emit(self, eng, fn, reads, writes, dma):
        toks = []
        for b in reads:
            toks.append(b.w)
        for b in writes:
            toks.append(b.w)
            toks.extend(b.r)
        need = {}
        for t in toks:
            if t is None:
                continue
            s, v = t
            if self.waited[eng].get(s, 0) < v:
                need[s] = max(need.get(s, 0), v)
        if dma:
            slot = self.dmaslot[eng]
            self.dmaslot[eng] = (slot + 1) % self.NPOOL
            semname = "D%s%d" % (eng, slot)
            prev = self.cnt[semname]
            if prev and self.waited[eng].get(semname, 0) < prev:
                need[semname] = max(need.get(semname, 0), prev)
            delta = 16
        else:
            semname = eng
            delta = 1
        self.cnt[semname] += delta
        tok = (semname, self.cnt[semname])
        sems = self.sems
        waited = self.waited[eng]

        def body(handle):
            for s, v in sorted(need.items()):
                handle.wait_ge(sems[s], v)
                waited[s] = max(waited.get(s, 0), v)
            fn(handle).then_inc(sems[semname], delta)

        getattr(self.block, self.ENG[eng])(body)
        for b in reads:
            b.r.append(tok)
        for b in writes:
            b.w = tok
            b.r = []
        return tok

    def op(self, eng, meth, *args, reads=(), writes=(), dma=False, **kw):
        return self._emit(eng, lambda h: getattr(h, meth)(*args, **kw),
                          reads, writes, dma)

    def raw(self, eng, fn, reads=(), writes=(), dma=False):
        return self._emit(eng, fn, reads, writes, dma)

    def barrier(self):
        toks = [(s, v) for s, v in self.cnt.items() if v > 0]
        for eng in self.ENG:
            self.fence(eng, [(s, v) for s, v in toks if s != eng])

    def fence(self, eng, toks):
        need = {}
        for t in toks:
            if t is None:
                continue
            s, v = t
            if self.waited[eng].get(s, 0) < v:
                need[s] = max(need.get(s, 0), v)
        if not need:
            return
        sems = self.sems
        waited = self.waited[eng]

        def body(handle):
            for s, v in sorted(need.items()):
                handle.wait_ge(sems[s], v)
                waited[s] = max(waited.get(s, 0), v)

        getattr(self.block, self.ENG[eng])(body)


def build_core_program(NT=4096, TA=256, TB=256, TC=256, n_cores=NC_,
                       exchange="cc", debug_dump=()):
    dbgset = set(debug_dump)
    NCH = NT // CH
    nc = bass.Bass("TRN2", target_bir_lowering=False, num_devices=n_cores)

    def ein(name, shape, dt=F32):
        return nc.dram_tensor(name, shape, dt, kind="ExternalInput").ap()

    x_in = ein("x", [NT, D])
    xh_in = ein("xh", [4, D])
    w_in_h = ein("w_in", [D, DTP])
    conv_w_h = ein("conv_w", [128, KC * XB])
    conv_b_h = ein("conv_b", [128, XB])
    dt_bias_h = ein("dt_bias", [H, 1])
    a_log_h = ein("a_log", [H, 1])
    d_h = ein("dvec", [1, H])
    norm_w_h = ein("norm_w", [128, ZB])
    w_out_h = ein("w_out", [DI, D])
    w_expT_h = ein("w_expT", [HD, D])
    w_conP_h = ein("w_conP", [HD, D])
    b_exp2_h = ein("b_exp2", [128, 32])
    up_bias_p_h = ein("up_bias_p", [128, HD // 128])
    b_out_h = ein("b_out", [1, D])
    down_b_h = ein("down_b", [1, D])
    bfv_h = ein("bfv", [128, 8 * NST * 32], BF16)
    ident_h = ein("ident", [128, 128], BF16)
    xpat_h = ein("xpat", [128, 8 * 128], BF16)
    tril_h = ein("tril", [128, CH])
    tb2_h = ein("tb2", [2, 128])
    ohE_h = ein("ohE", [H, 16 * 128])
    smask_h = ein("smask", [128, 8])

    out_ap = nc.dram_tensor("out", [NT, D], F32, kind="ExternalOutput").ap()

    def dscr(name, shape, dt):
        return nc.dram_tensor(name, shape, dt).ap()

    sp_z = dscr("sp_z", [DI, NT], BF16)
    sp_xs = dscr("sp_xs", [NT, DI], BF16)
    sp_xsw = dscr("sp_xsw", [NT, DI], BF16)
    sp_bfm = dscr("sp_bfm", [DS, NT], BF16)
    sp_btm = dscr("sp_btm", [NT, DS], BF16)
    sp_cfm = dscr("sp_cfm", [DS, NT], BF16)
    sp_x1 = dscr("sp_x1", [NT, D], F32)
    sp_w = dscr("sp_w", [HD, D], BF16)
    sp_w2 = dscr("sp_w2", [HD, D], BF16)
    sp_g2 = dscr("sp_g2", [H, 512], F32)
    sp_bv = dscr("sp_bv", [HD, 1], F32)
    cc_in = dscr("cc_in", [DS, DI], F32)
    cc_out = dscr("cc_out", [n_cores * DS, DI], F32)
    bS = {}
    for _n in ("sp_z", "sp_xs", "sp_xsw", "sp_bfm", "sp_btm", "sp_cfm",
               "sp_x1", "sp_w", "sp_w2", "sp_g2", "sp_bv", "cc_in", "cc_out"):
        bS[_n] = Buf(_n)
    if exchange == "none":
        sin_h = ein("sin", [DS, DI], F32)
        sfin_t = nc.dram_tensor("sfin", [DS, DI], F32, kind="ExternalOutput").ap()

    import contextlib
    with contextlib.ExitStack() as _es:
        sems = {}
        for e in ("te", "act", "dve", "gps", "sp"):
            sems[e] = _es.enter_context(nc.semaphore("s_" + e))
        for q in ("sp", "gps", "act"):
            for i in range(Prog.NPOOL):
                sems["D%s%d" % (q, i)] = _es.enter_context(
                    nc.semaphore("sD%s%d" % (q, i)))
        block = _es.enter_context(nc.Block())
        p = Prog(nc, block, sems)

        def sb(name, shape, dt, stack=None):
            if stack is None:
                t = nc.alloc_sbuf_tensor(name, shape, dt)
            else:
                t = stack.enter_context(nc.sbuf_tensor(name, shape, dt))
            return t, Buf(name)

        cid, b_cid = sb("c_id", [128, 128], BF16)
        cidf, b_cidf = sb("c_idf", [128, 128], F32)
        co1, b_co1 = sb("c_o1", [1, 128], BF16)
        co1f, b_co1f = sb("c_o1f", [1, 128], F32)
        coK, b_coK = sb("c_oK", [128, 1], BF16)
        stat, b_stat = sb("stat", [128, 24], F32)
        rowa, b_rowa = sb("rowa", [1, 64], F32)
        boutsb, b_bout = sb("boutsb", [1, D], BF16)
        _esM = contextlib.ExitStack()
        ctril, b_ctril = sb("c_tril", [128, CH], F32, _esM)
        ctb2, b_ctb2 = sb("c_tb2", [2, 128], F32, _esM)
        czs, b_czs = sb("c_zs", [H, 512], F32, _esM)
        convw, b_convw = sb("p_cw", [128, KC * XB], F32, _esM)
        convb, b_convb = sb("p_cb", [128, XB], F32, _esM)
        dtbias, b_dtbias = sb("p_dtb", [H, 1], F32, _esM)
        aneg, b_aneg = sb("p_aneg", [H, 1], F32, _esM)
        dvecs, b_dvecs = sb("p_dv", [1, H], F32, _esM)
        normw, b_normw = sb("p_nw", [128, ZB], F32, _esM)
        smk, b_smk = sb("p_smk", [128, 8], F32, _esM)
        gfm, b_gfm = sb("gfm", [H, NT], F32, _esM)
        lndt, b_lndt = sb("lndt", [H, NT], F32, _esM)
        dcrow, b_dcrow = sb("dcrow", [1, NCH * H], F32, _esM)
        gcar, b_gcar = sb("gcar", [H, 1], F32, _esM)
        dcprev, b_dcprev = sb("dcprev", [H, 1], F32, _esM)
        S, b_S = sb("S", [DS, DI], F32, _esM)
        Sb_, b_Sb = sb("Sb", [DS, DI], BF16, _esM)
        St, b_St = sb("St", [DS, DI], F32, _esM)
        ddiag, b_ddiag = sb("ddiag", [128, H * CH], BF16, _esM)
        drep, b_drep = sb("drep", [128, H], F32, _esM)
        rowz, b_rowz = sb("rowz", [1, 1024], BF16, _esM)

        ps0 = nc.alloc_psum_tensor("ps0", [128, 1024], F32)
        ps1 = nc.alloc_psum_tensor("ps1", [128, 1024], F32)
        ps2 = nc.alloc_psum_tensor("ps2", [128, 1024], F32)
        pt0 = nc.alloc_psum_tensor("pt0", [128, 1024], BF16)
        pt1 = nc.alloc_psum_tensor("pt1", [128, 1024], BF16)
        b_ps0, b_ps1, b_ps2 = Buf("ps0"), Buf("ps1"), Buf("ps2")
        b_pt0, b_pt1 = Buf("pt0"), Buf("pt1")
        PT = [(pt0, b_pt0), (pt1, b_pt1)]

        def ld(dst, bdst, src):
            p.op("gps", "dma_start", out=dst[:], in_=src, writes=[bdst], dma=True)

        ld(cid, b_cid, ident_h)
        p.op("dve", "tensor_copy", out=cidf[:], in_=cid[:], reads=[b_cid],
             writes=[b_cidf])
        ld(ctril, b_ctril, tril_h)
        ld(ctb2, b_ctb2, tb2_h)
        ld(convw, b_convw, conv_w_h)
        ld(convb, b_convb, conv_b_h)
        ld(dtbias, b_dtbias, dt_bias_h)
        ld(aneg, b_aneg, a_log_h)
        ld(dvecs, b_dvecs, d_h)
        ld(normw, b_normw, norm_w_h)
        ld(smk, b_smk, smask_h)
        p.op("dve", "memset", co1[:], 1.0, writes=[b_co1])
        p.op("dve", "memset", co1f[:], 1.0, writes=[b_co1f])
        p.op("dve", "memset", coK[:], 1.0, writes=[b_coK])
        p.op("dve", "memset", czs[:], 0.0, writes=[b_czs])
        p.op("dve", "memset", gcar[:], 0.0, writes=[b_gcar])
        p.op("dve", "memset", dcprev[:], 0.0, writes=[b_dcprev])
        p.op("dve", "memset", S[:], 0.0, writes=[b_S])
        p.op("act", "activation", aneg[:], aneg[:], AF.Exp,
             reads=[b_aneg], writes=[b_aneg])
        p.op("dve", "tensor_scalar_mul", aneg[:], aneg[:], -1.0,
             reads=[b_aneg], writes=[b_aneg])
        p.op("gps", "dma_start", out=boutsb[:], in_=b_out_h, writes=[b_bout], dma=True)
        p.op("gps", "dma_start", out=rowz[:], in_=down_b_h, writes=[b_rowz], dma=True)
        p.op("dve", "tensor_add", out=boutsb[:], in0=boutsb[:], in1=rowz[:],
             reads=[b_rowz], writes=[b_bout])
        p.raw("te", lambda t: t.matmul(
            ps2[:, 0:H], co1f[:], dvecs[0:1, 0:H],
            start=True, stop=True),
            reads=[b_co1f, b_dvecs], writes=[b_ps2])
        p.op("dve", "tensor_copy", out=drep[:], in_=ps2[:, 0:H],
             reads=[b_ps2], writes=[b_drep])
        p.raw("dve", lambda v: v.tensor_tensor(
            out=AP(ddiag, 0, [[H * CH, 64], [CH, H], [1, CH]]),
            in0=AP(cid, 7 * 0 + 0, [[128, 64], [0, H], [1, 64]]),
            in1=AP(drep, 0, [[H, 64], [1, H], [0, CH]]), op=ALU.mult),
            reads=[b_cid, b_drep], writes=[b_ddiag])
        p.raw("dve", lambda v: v.tensor_tensor(
            out=AP(ddiag, 64 * (H * CH), [[H * CH, 64], [CH, H], [1, CH]]),
            in0=AP(cid, 64 * 128 + 64, [[128, 64], [0, H], [1, 64]]),
            in1=AP(drep, 64 * H, [[H, 64], [1, H], [0, CH]]), op=ALU.mult),
            reads=[b_cid, b_drep], writes=[b_ddiag])

        def ln_tm(src, b_src, dst, b_dst, np_, width, soff):
            sums = stat[0:np_, 0:1]
            ssq = stat[0:np_, 1:2]
            mu = stat[0:np_, 2:3]
            ivar = stat[0:np_, 3:4]
            scr = stat[0:np_, 4:5]
            p.op("dve", "reduce_sum", sums, src[0:np_, soff:soff + width],
                 axis=mybir.AxisListType.X, reads=[b_src], writes=[b_stat])
            p.op("act", "activation", dst[0:np_, soff:soff + width],
                 src[0:np_, soff:soff + width], AF.Square, accum_out=ssq,
                 reads=[b_src], writes=[b_dst, b_stat])
            inv_n = 1.0 / width
            p.op("dve", "tensor_scalar_mul", mu, sums, inv_n,
                 reads=[b_stat], writes=[b_stat])
            p.op("dve", "tensor_mul", out=scr, in0=mu, in1=mu,
                 reads=[b_stat], writes=[b_stat])
            p.op("dve", "tensor_scalar_mul", ivar, ssq, inv_n,
                 reads=[b_stat], writes=[b_stat])
            p.op("dve", "tensor_sub", out=ivar, in0=ivar, in1=scr,
                 reads=[b_stat], writes=[b_stat])
            p.op("dve", "tensor_scalar_add", ivar, ivar, EPS,
                 reads=[b_stat], writes=[b_stat])
            p.op("act", "activation", ivar, ivar, AF.Ln,
                 reads=[b_stat], writes=[b_stat])
            p.op("act", "activation", ivar, ivar, AF.Exp, scale=-0.5,
                 reads=[b_stat], writes=[b_stat])
            p.op("dve", "tensor_scalar", out=dst[0:np_, soff:soff + width],
                 in0=src[0:np_, soff:soff + width], scalar1=mu, scalar2=ivar,
                 op0=ALU.subtract, op1=ALU.mult,
                 reads=[b_src, b_stat], writes=[b_dst])

        NTT_A = NT // TA
        nblk = TA // 128
        with contextlib.ExitStack() as _esA:
            winsb = _esA.enter_context(nc.sbuf_tensor("winsb", [128, 8 * DTP], BF16))
            xt = _esA.enter_context(nc.sbuf_tensor("xt", [128, nblk * D], F32))
            xnt = _esA.enter_context(nc.sbuf_tensor("xnt", [128, nblk * D], BF16))
            xnf = _esA.enter_context(nc.sbuf_tensor("xnf", [128, 8 * TA], BF16))
            zsl = _esA.enter_context(nc.sbuf_tensor("zsl", [128, ZB * TA], BF16))
            cvo = _esA.enter_context(nc.sbuf_tensor("cvo", [128, TA], F32))
            xsf = _esA.enter_context(nc.sbuf_tensor("xsf", [128, 16 * TA], BF16))
            bcf = _esA.enter_context(nc.sbuf_tensor("bcf", [128, 2 * TA], BF16))
            xtl = _esA.enter_context(nc.sbuf_tensor("xtl", [128, XB * 3], F32))
            xstm = _esA.enter_context(nc.sbuf_tensor("xstm", [128, nblk * DI], BF16))
            xswt = _esA.enter_context(nc.sbuf_tensor("xswt", [128, nblk * DI], BF16))
            btm = _esA.enter_context(nc.sbuf_tensor("btm", [128, nblk * DS], BF16))
            dts = _esA.enter_context(nc.sbuf_tensor("dts", [H, TA * 4 + 256], F32))
            wtp = _esA.enter_context(nc.sbuf_tensor("wtp", [128, nblk * H], F32))
            dctt = _esA.enter_context(nc.sbuf_tensor("dctt", [8, H], F32))
            b_win, b_xt, b_xnt, b_xnf = Buf("win"), Buf("xt"), Buf("xnt"), Buf("xnf")
            b_zsl, b_cvo, b_xsf = Buf("zsl"), Buf("cvo"), Buf("xsf")
            b_bcf, b_xtl = Buf("bcf"), Buf("xtl")
            b_xstm, b_xswt, b_btm = Buf("xstm"), Buf("xswt"), Buf("btm")
            b_dts, b_wtp, b_dctt = Buf("dts"), Buf("wtp"), Buf("dctt")

            for k in range(8):
                p.raw("gps", lambda g, k=k: g.dma_start(
                    out=winsb[:, k * DTP:(k + 1) * DTP],
                    in_=w_in_h[k * 128:(k + 1) * 128, :]),
                    writes=[b_win], dma=True)

            p.op("sp", "dma_start", out=xt[0:4, 0:D], in_=xh_in,
                 writes=[b_xt], dma=True)
            ln_tm(xt, b_xt, xnt, b_xnt, 4, D, 0)
            for k in range(8):
                pt, b_pt = PT[k % 2]
                p.op("dve", "tensor_copy", out=xnf[0:4, k * TA:k * TA + 128],
                     in_=xnt[0:4, k * 128:(k + 1) * 128],
                     reads=[b_xnt], writes=[b_xnf])
                p.op("te", "transpose", pt[:, 0:4],
                     xnf[0:4, k * TA:k * TA + 128], cid[0:4, 0:4],
                     reads=[b_xnf, b_cid], writes=[b_pt])
                p.op("dve", "tensor_copy", out=xnf[:, k * TA + TA - 4:k * TA + TA],
                     in_=pt[:, 0:4], reads=[b_pt], writes=[b_xnf])
            for fb in range(XB):
                fc = DI + fb * 128
                for k in range(8):
                    p.op("te", "matmul", ps0[:, 0:4],
                         winsb[:, k * DTP + fc:k * DTP + fc + 128],
                         xnf[:, k * TA + TA - 4:k * TA + TA],
                         start=(k == 0), stop=(k == 7),
                         reads=[b_win, b_xnf], writes=[b_ps0])
                p.op("dve", "tensor_copy", out=xtl[:, fb * 3:fb * 3 + 3],
                     in_=ps0[:, 0:3], reads=[b_ps0], writes=[b_xtl])

            for it in range(NTT_A):
                t0 = it * TA
                for j in range(nblk):
                    p.raw("sp", lambda s, j=j, t0=t0: s.dma_start(
                        out=xt[:, j * D:(j + 1) * D],
                        in_=x_in[t0 + j * 128:t0 + (j + 1) * 128, :]),
                        writes=[b_xt], dma=True)
                for j in range(nblk):
                    ln_tm(xt, b_xt, xnt, b_xnt, 128, D, j * D)
                for j in range(nblk):
                    for k in range(8):
                        pt, b_pt = PT[(j * 8 + k) % 2]
                        p.op("te", "transpose", pt[:, 0:128],
                             xnt[:, j * D + k * 128:j * D + (k + 1) * 128],
                             cid[:], reads=[b_xnt, b_cid], writes=[b_pt])
                        p.op("dve", "tensor_copy",
                             out=xnf[:, k * TA + j * 128:k * TA + (j + 1) * 128],
                             in_=pt[:, 0:128], reads=[b_pt], writes=[b_xnf])
                slots = [(ps0, b_ps0, 0), (ps0, b_ps0, 512),
                         (ps1, b_ps1, 0), (ps1, b_ps1, 512)]
                si = 0
                for fb in range(ZB):
                    psx, b_psx, off = slots[si]; si = (si + 1) % 4
                    for k in range(8):
                        p.op("te", "matmul", psx[:, off:off + TA],
                             winsb[:, k * DTP + fb * 128:k * DTP + (fb + 1) * 128],
                             xnf[:, k * TA:(k + 1) * TA],
                             start=(k == 0), stop=(k == 7), skip_group_check=True,
                             reads=[b_win, b_xnf], writes=[b_psx])
                    p.op("act", "activation", zsl[:, fb * TA:(fb + 1) * TA],
                         psx[:, off:off + TA], AF.Silu,
                         reads=[b_psx], writes=[b_zsl])
                for fb in range(XB):
                    fc = DI + fb * 128
                    psx, b_psx, off = slots[si]; si = (si + 1) % 4
                    for k in range(8):
                        p.op("te", "matmul", psx[:, off:off + TA],
                             winsb[:, k * DTP + fc:k * DTP + fc + 128],
                             xnf[:, k * TA:(k + 1) * TA],
                             start=(k == 0), stop=(k == 7), skip_group_check=True,
                             reads=[b_win, b_xnf], writes=[b_psx])
                    wcol = lambda k_: convw[:, k_ * XB + fb:k_ * XB + fb + 1]
                    p.op("dve", "tensor_scalar_mul", cvo[:], psx[:, off:off + TA],
                         wcol(3), reads=[b_psx, b_convw], writes=[b_cvo])
                    for k in range(3):
                        sh = 3 - k
                        p.op("dve", "scalar_tensor_tensor",
                             out=cvo[:, sh:TA], in0=psx[:, off:off + TA - sh],
                             scalar=wcol(k), in1=cvo[:, sh:TA],
                             op0=ALU.mult, op1=ALU.add,
                             reads=[b_psx, b_convw, b_cvo], writes=[b_cvo])
                        p.op("dve", "scalar_tensor_tensor",
                             out=cvo[:, 0:sh],
                             in0=xtl[:, fb * 3 + 3 - sh:fb * 3 + 3],
                             scalar=wcol(k), in1=cvo[:, 0:sh],
                             op0=ALU.mult, op1=ALU.add,
                             reads=[b_xtl, b_convw, b_cvo], writes=[b_cvo])
                    p.op("dve", "tensor_copy", out=xtl[:, fb * 3:fb * 3 + 3],
                         in_=psx[:, off + TA - 3:off + TA],
                         reads=[b_psx], writes=[b_xtl])
                    dst = (xsf[:, fb * TA:(fb + 1) * TA] if fb < 16 else
                           bcf[:, (fb - 16) * TA:(fb - 15) * TA])
                    wdst = b_xsf if fb < 16 else b_bcf
                    p.op("act", "activation", dst, cvo[:], AF.Silu,
                         bias=convb[:, fb:fb + 1],
                         reads=[b_cvo, b_convb], writes=[wdst])
                psd, b_psd, offd = slots[si]; si = (si + 1) % 4
                for k in range(8):
                    p.op("te", "matmul", psd[0:H, offd:offd + TA],
                         winsb[:, k * DTP + 4352:k * DTP + 4384],
                         xnf[:, k * TA:(k + 1) * TA],
                         start=(k == 0), stop=(k == 7), skip_group_check=True,
                         reads=[b_win, b_xnf], writes=[b_psd])
                u = dts[:, 0:TA]
                v1 = dts[:, TA:2 * TA]
                w2 = dts[:, 2 * TA:3 * TA]
                dtv = dts[:, 3 * TA:4 * TA]
                xtr = dts[:, 4 * TA:4 * TA + 256]
                p.op("dve", "tensor_scalar_add", u, psd[0:H, offd:offd + TA],
                     dtbias[:], reads=[b_psd, b_dtbias], writes=[b_dts])
                p.op("act", "activation", v1, u, AF.Abs,
                     reads=[b_dts], writes=[b_dts])
                p.op("act", "activation", v1, v1, AF.Exp, scale=-1.0,
                     reads=[b_dts], writes=[b_dts])
                p.op("act", "activation", v1, v1, AF.Ln, bias=1.0,
                     reads=[b_dts], writes=[b_dts])
                p.op("act", "activation", w2, u, AF.Relu,
                     reads=[b_dts], writes=[b_dts])
                p.op("dve", "tensor_add", out=dtv, in0=v1, in1=w2,
                     reads=[b_dts], writes=[b_dts])
                p.op("act", "activation", lndt[:, t0:t0 + TA], dtv, AF.Ln,
                     reads=[b_dts], writes=[b_lndt])
                p.op("dve", "tensor_scalar_mul", u, dtv, aneg[:],
                     reads=[b_dts, b_aneg], writes=[b_dts])
                p.op("dve", "tensor_tensor_scan", out=gfm[:, t0:t0 + TA],
                     data0=u, data1=czs[:, 0:TA], initial=gcar[:],
                     op0=ALU.add, op1=ALU.add,
                     reads=[b_dts, b_czs, b_gcar], writes=[b_gfm])
                p.op("dve", "tensor_copy", out=gcar[:],
                     in_=gfm[:, t0 + TA - 1:t0 + TA],
                     reads=[b_gfm], writes=[b_gcar])
                nct = TA // CH
                c0 = t0 // CH
                p.op("dve", "tensor_copy", out=xtr[:, 0:1], in_=dcprev[:],
                     reads=[b_dcprev], writes=[b_dts])
                if nct > 1:
                    p.raw("dve", lambda v, t0=t0, nct=nct: v.tensor_copy(
                        out=xtr[:, 1:nct],
                        in_=AP(gfm, t0 + CH - 1, [[NT, H], [CH, nct - 1]])),
                        reads=[b_gfm], writes=[b_dts])
                p.raw("dve", lambda v, t0=t0, nct=nct: v.tensor_sub(
                    out=xtr[:, 8:8 + nct],
                    in0=AP(gfm, t0 + CH - 1, [[NT, H], [CH, nct]]),
                    in1=xtr[:, 0:nct]),
                    reads=[b_gfm, b_dts], writes=[b_dts])
                p.op("act", "activation", xtr[:, 8:8 + nct], xtr[:, 8:8 + nct],
                     AF.Exp, reads=[b_dts], writes=[b_dts])
                p.raw("dve", lambda v, t0=t0: v.tensor_copy(
                    out=dcprev[:], in_=AP(gfm, t0 + TA - 1, [[NT, H], [1, 1]])),
                    reads=[b_gfm], writes=[b_dcprev])
                p.op("te", "transpose", ps2[0:nct, 896:896 + H],
                     xtr[:, 8:8 + nct], cidf[0:H, 0:H],
                     reads=[b_dts, b_cidf], writes=[b_ps2])
                p.op("dve", "tensor_copy", out=dctt[0:nct, :],
                     in_=ps2[0:nct, 896:896 + H], reads=[b_ps2], writes=[b_dctt])
                p.raw("gps", lambda g, c0=c0, nct=nct: g.dma_start(
                    out=dcrow[0:1, c0 * H:(c0 + nct) * H],
                    in_=dctt[0:nct, :]),
                    reads=[b_dctt], writes=[b_dcrow], dma=True)
                p.raw("dve", lambda v, t0=t0, nct=nct: v.tensor_sub(
                    out=w2,
                    in0=AP(gfm, t0 + CH - 1, [[NT, H], [CH, nct], [0, CH]]),
                    in1=gfm[:, t0:t0 + TA]),
                    reads=[b_gfm], writes=[b_dts])
                p.op("act", "activation", w2, w2, AF.Exp,
                     reads=[b_dts], writes=[b_dts])
                p.op("dve", "tensor_mul", out=w2, in0=w2, in1=dtv,
                     reads=[b_dts], writes=[b_dts])

                for j in range(nblk):
                    for cb in range(16):
                        pt, b_pt = PT[cb % 2]
                        p.op("te", "transpose", pt[:, 0:128],
                             xsf[:, cb * TA + j * 128:cb * TA + (j + 1) * 128],
                             cid[:], reads=[b_xsf, b_cid], writes=[b_pt])
                        p.op("dve", "tensor_copy",
                             out=xstm[:, j * DI + cb * 128:j * DI + (cb + 1) * 128],
                             in_=pt[:, 0:128], reads=[b_pt], writes=[b_xstm])
                    pt, b_pt = PT[0]
                    p.op("te", "transpose", pt[:, 0:128],
                         bcf[:, j * 128:(j + 1) * 128], cid[:],
                         reads=[b_bcf, b_cid], writes=[b_pt])
                    p.op("dve", "tensor_copy", out=btm[:, j * DS:(j + 1) * DS],
                         in_=pt[:, 0:128], reads=[b_pt], writes=[b_btm])
                    p.op("te", "transpose", ps2[0:128, 896:896 + H],
                         w2[:, j * 128:(j + 1) * 128], cidf[0:H, 0:H],
                         reads=[b_dts, b_cidf], writes=[b_ps2])
                    p.op("dve", "tensor_copy", out=wtp[:, j * H:(j + 1) * H],
                         in_=ps2[0:128, 896:896 + H], reads=[b_ps2], writes=[b_wtp])
                    p.raw("dve", lambda v, j=j: v.tensor_tensor(
                        out=xswt[:, j * DI:(j + 1) * DI],
                        in0=xstm[:, j * DI:(j + 1) * DI],
                        in1=AP(wtp, j * H, [[nblk * H, 128], [1, H], [0, PH]]),
                        op=ALU.mult),
                        reads=[b_xstm, b_wtp], writes=[b_xswt])
                    for cc in range(2):
                        c_glob = (t0 + j * 128) // CH + cc
                        for q in range(4):
                            pstt = ps1 if q < 2 else ps2
                            b_pstt = b_ps1 if q < 2 else b_ps2
                            p.op("te", "matmul",
                                 pstt[:, (q % 2) * 512:(q % 2 + 1) * 512],
                                 btm[cc * 64:(cc + 1) * 64, j * DS:(j + 1) * DS],
                                 xswt[cc * 64:(cc + 1) * 64,
                                      j * DI + q * 512:j * DI + (q + 1) * 512],
                                 start=True, stop=True, tile_position=(cc * 64, 0),
                                 reads=[b_btm, b_xswt], writes=[b_pstt])
                        p.raw("te", lambda t, c_glob=c_glob: t.matmul(
                            ps0[:, 0:H], co1f[:],
                            dcrow[0:1, c_glob * H:(c_glob + 1) * H],
                            start=True, stop=True),
                            reads=[b_co1f, b_dcrow], writes=[b_ps0])
                        p.raw("dve", lambda v: v.tensor_tensor(
                            out=St[:], in0=S[:],
                            in1=AP(ps0, 0, [[1024, 128], [1, H], [0, PH]]),
                            op=ALU.mult),
                            reads=[b_S, b_ps0], writes=[b_St])
                        p.raw("dve", lambda v: v.tensor_add(
                            out=S[:, 0:1024], in0=St[:, 0:1024], in1=ps1[:, :]),
                            reads=[b_St, b_ps1], writes=[b_S])
                        p.raw("dve", lambda v: v.tensor_add(
                            out=S[:, 1024:2048], in0=St[:, 1024:2048], in1=ps2[:, :]),
                            reads=[b_St, b_ps2], writes=[b_S])
                    gt = t0 + j * 128
                    p.raw("sp", lambda s, j=j, gt=gt: s.dma_start(
                        out=sp_xs[gt:gt + 128, :],
                        in_=xstm[:, j * DI:(j + 1) * DI]),
                        reads=[b_xstm], writes=[bS["sp_xs"]], dma=True)
                    p.raw("sp", lambda s, j=j, gt=gt: s.dma_start(
                        out=sp_xsw[gt:gt + 128, :],
                        in_=xswt[:, j * DI:(j + 1) * DI]),
                        reads=[b_xswt], writes=[bS["sp_xsw"]], dma=True)
                    p.raw("act", lambda s, j=j, gt=gt: s.dma_start(
                        out=sp_btm[gt:gt + 128, :],
                        in_=btm[:, j * DS:(j + 1) * DS]),
                        reads=[b_btm], writes=[bS["sp_btm"]], dma=True)
                p.raw("act", lambda s, t0=t0: s.dma_start(
                    out=AP(sp_z.tensor, t0, [[NT, 128], [128 * NT, ZB], [1, TA]]),
                    in_=zsl[:]), reads=[b_zsl], writes=[bS["sp_z"]], dma=True)
                p.raw("act", lambda s, t0=t0: s.dma_start(
                    out=sp_bfm[:, t0:t0 + TA], in_=bcf[:, 0:TA]),
                    reads=[b_bcf], writes=[bS["sp_bfm"]], dma=True)
                p.raw("act", lambda s, t0=t0: s.dma_start(
                    out=sp_cfm[:, t0:t0 + TA], in_=bcf[:, TA:2 * TA]),
                    reads=[b_bcf], writes=[bS["sp_cfm"]], dma=True)

        p.barrier()
        if exchange == "cc":
            p.op("sp", "dma_start", out=cc_in, in_=S[:], reads=[b_S],
                 writes=[bS["cc_in"]], dma=True)
            b_cc = Buf("cc")
            p.raw("gps", lambda g: g.collective_compute(
                "AllGather", ALU.bypass, replica_groups=[list(range(n_cores))],
                ins=[cc_in], outs=[cc_out]),
                reads=[bS["cc_in"]], writes=[bS["cc_out"]])
            p.op("dve", "memset", S[:], 0.0, writes=[b_S])
            gsl, b_gsl = sb("gslot", [DS, DI], F32, _esM)
            for k in range(n_cores):
                p.raw("sp", lambda s, k=k: s.dma_start(
                    out=gsl[:], in_=cc_out[k * DS:(k + 1) * DS, :]),
                    reads=[bS["cc_out"]], writes=[b_gsl], dma=True)
                p.op("dve", "scalar_tensor_tensor", out=S[:], in0=gsl[:],
                     scalar=smk[:, k:k + 1], in1=S[:], op0=ALU.mult, op1=ALU.add,
                     reads=[b_gsl, b_smk], writes=[b_S])
        else:
            p.op("sp", "dma_start", out=sfin_t, in_=S[:], reads=[b_S], dma=True)
            p.op("sp", "dma_start", out=S[:], in_=sin_h, writes=[b_S], dma=True)
            p.op("dve", "tensor_scalar_mul", S[:], S[:], smk[:, 0:1],
                 reads=[b_S, b_smk], writes=[b_S])

        NTT_B = NT // TB
        npb = TB // 128
        with contextlib.ExitStack() as _esB:
            wosb = _esB.enter_context(nc.sbuf_tensor("wosb", [128, ZB * D], BF16))
            cohE = _esB.enter_context(nc.sbuf_tensor("c_ohE", [H, 16 * 128], F32))
            xstm2 = _esB.enter_context(nc.sbuf_tensor("xstm2", [128, npb * DI], BF16))
            xsw2 = _esB.enter_context(nc.sbuf_tensor("xsw2", [128, npb * DI], BF16))
            silz2 = _esB.enter_context(nc.sbuf_tensor("silz2", [128, ZB * TB], BF16))
            cf2 = _esB.enter_context(nc.sbuf_tensor("cf2", [128, TB], BF16))
            bf2 = _esB.enter_context(nc.sbuf_tensor("bf2", [128, TB], BF16))
            bt2 = _esB.enter_context(nc.sbuf_tensor("bt2", [128, npb * DS], BF16))
            xt2 = _esB.enter_context(nc.sbuf_tensor("xt2", [128, npb * D], F32))
            erep = _esB.enter_context(nc.sbuf_tensor("erep", [128, ZB * TB], BF16))
            et2 = _esB.enter_context(nc.sbuf_tensor("et2", [H, TB + 64], F32))
            qtm = _esB.enter_context(nc.sbuf_tensor("qtm", [128, npb * H], F32))
            grw = _esB.enter_context(nc.sbuf_tensor("grw", [2, DI], F32))
            Ld = _esB.enter_context(nc.sbuf_tensor("Ld", [128, DI], BF16))
            M3 = _esB.enter_context(nc.sbuf_tensor("M3", [128, DI], BF16))
            gm2 = _esB.enter_context(nc.sbuf_tensor("gm2", [128, 2 * CH], BF16))
            uscr = _esB.enter_context(nc.sbuf_tensor("uscr", [128, DI], F32))
            yscr = _esB.enter_context(nc.sbuf_tensor("yscr", [128, 1024], F32))
            yg2 = _esB.enter_context(nc.sbuf_tensor("yg2", [128, 1024], BF16))
            yn2 = _esB.enter_context(nc.sbuf_tensor("yn2", [128, ZB * TB], BF16))
            qfm = _esB.enter_context(nc.sbuf_tensor("qfm", [H, 128], F32))
            b_wo, b_xstm2, b_xsw2 = Buf("wo"), Buf("xstm2"), Buf("xsw2")
            b_cohE = Buf("cohE")
            ld(cohE, b_cohE, ohE_h)
            b_silz2, b_cf2, b_bf2, b_bt2 = Buf("silz2"), Buf("cf2"), Buf("bf2"), Buf("bt2")
            b_xt2, b_erep, b_et2 = Buf("xt2"), Buf("erep"), Buf("et2")
            b_qtm, b_grw, b_Ld, b_M3 = Buf("qtm"), Buf("grw"), Buf("Ld"), Buf("M3")
            b_gm2, b_uscr, b_yscr = Buf("gm2"), Buf("uscr"), Buf("yscr")
            b_yg2, b_yn2, b_qfm = Buf("yg2"), Buf("yn2"), Buf("qfm")

            for kb in range(ZB):
                p.raw("gps", lambda g, kb=kb: g.dma_start(
                    out=wosb[:, kb * D:(kb + 1) * D],
                    in_=w_out_h[kb * 128:(kb + 1) * 128, :]),
                    writes=[b_wo], dma=True)
            for kb in range(ZB):
                p.op("dve", "tensor_scalar_mul",
                     wosb[:, kb * D:(kb + 1) * D], wosb[:, kb * D:(kb + 1) * D],
                     normw[:, kb:kb + 1],
                     reads=[b_wo, b_normw], writes=[b_wo])

            for it in range(NTT_B):
                t0 = it * TB
                for j in range(npb):
                    gt = t0 + j * 128
                    p.raw("sp", lambda s, j=j, gt=gt: s.dma_start(
                        out=xstm2[:, j * DI:(j + 1) * DI], in_=sp_xs[gt:gt + 128, :]),
                        reads=[bS["sp_xs"]], writes=[b_xstm2], dma=True)
                    p.raw("sp", lambda s, j=j, gt=gt: s.dma_start(
                        out=xsw2[:, j * DI:(j + 1) * DI], in_=sp_xsw[gt:gt + 128, :]),
                        reads=[bS["sp_xsw"]], writes=[b_xsw2], dma=True)
                    p.raw("sp", lambda s, j=j, gt=gt: s.dma_start(
                        out=bt2[:, j * DS:(j + 1) * DS], in_=sp_btm[gt:gt + 128, :]),
                        reads=[bS["sp_btm"]], writes=[b_bt2], dma=True)
                    p.raw("act", lambda s, j=j, gt=gt: s.dma_start(
                        out=xt2[:, j * D:(j + 1) * D], in_=x_in[gt:gt + 128, :]),
                        writes=[b_xt2], dma=True)
                p.raw("act", lambda s, t0=t0: s.dma_start(
                    out=AP(silz2, 0, [[ZB * TB, 128], [TB, ZB], [1, TB]]),
                    in_=AP(sp_z.tensor, t0, [[NT, 128], [128 * NT, ZB], [1, TB]])),
                    reads=[bS["sp_z"]], writes=[b_silz2], dma=True)
                p.raw("act", lambda s, t0=t0: s.dma_start(
                    out=cf2[:], in_=sp_cfm[:, t0:t0 + TB]),
                    reads=[bS["sp_cfm"]], writes=[b_cf2], dma=True)
                p.raw("act", lambda s, t0=t0: s.dma_start(
                    out=bf2[:], in_=sp_bfm[:, t0:t0 + TB]),
                    reads=[bS["sp_bfm"]], writes=[b_bf2], dma=True)
                nct = TB // CH
                c0 = t0 // CH
                gs = et2[:, TB:TB + nct]
                if t0 == 0:
                    p.op("dve", "memset", gs[:, 0:1], 0.0, writes=[b_et2])
                else:
                    p.raw("dve", lambda v, t0=t0: v.tensor_copy(
                        out=gs[:, 0:1], in_=gfm[:, t0 - 1:t0]),
                        reads=[b_gfm], writes=[b_et2])
                if nct > 1:
                    p.raw("dve", lambda v, t0=t0, nct=nct: v.tensor_copy(
                        out=gs[:, 1:nct],
                        in_=AP(gfm, t0 + CH - 1, [[NT, H], [CH, nct - 1]])),
                        reads=[b_gfm], writes=[b_et2])
                p.raw("dve", lambda v, t0=t0, nct=nct: v.tensor_tensor(
                    out=et2[:, 0:TB], in0=gfm[:, t0:t0 + TB],
                    in1=AP(et2, TB, [[TB + 64, H], [1, nct], [0, CH]]),
                    op=ALU.subtract),
                    reads=[b_gfm, b_et2], writes=[b_et2])
                p.op("act", "activation", et2[:, 0:TB], et2[:, 0:TB], AF.Exp,
                     reads=[b_et2], writes=[b_et2])
                for b in range(ZB):
                    p.op("te", "matmul", ps2[:, 0:TB],
                         cohE[:, b * 128:(b + 1) * 128], et2[:, 0:TB],
                         start=True, stop=True,
                         reads=[b_cohE, b_et2], writes=[b_ps2])
                    p.op("dve", "tensor_copy", out=erep[:, b * TB:(b + 1) * TB],
                         in_=ps2[:, 0:TB], reads=[b_ps2], writes=[b_erep])
                p.raw("act", lambda s, t0=t0: s.dma_start(
                    out=sp_g2[:, 0:TB], in_=gfm[:, t0:t0 + TB]),
                    reads=[b_gfm], writes=[bS["sp_g2"]], dma=True)


                for j in range(npb):
                    tp0 = t0 + j * 128
                    for cc_ in range(2):
                        p.raw("sp", lambda s, cc_=cc_, j=j: s.dma_start(
                            out=AP(grw, cc_ * DI, [[DI, 1], [CH, H], [1, CH]]),
                            in_=AP(sp_g2.tensor, j * 128 + cc_ * CH,
                                   [[512, H], [1, CH]])),
                            reads=[bS["sp_g2"]], writes=[b_grw], dma=True)
                    p.raw("dve", lambda v, tp0=tp0: v.tensor_sub(
                        out=qfm[:], in0=lndt[:, tp0:tp0 + 128],
                        in1=gfm[:, tp0:tp0 + 128]),
                        reads=[b_lndt, b_gfm], writes=[b_qfm])
                    p.op("te", "transpose", ps2[0:128, 896:896 + H], qfm[:],
                         cidf[0:H, 0:H], reads=[b_qfm, b_cidf], writes=[b_ps2])
                    p.op("dve", "tensor_copy", out=qtm[:, j * H:(j + 1) * H],
                         in_=ps2[0:128, 896:896 + H], reads=[b_ps2], writes=[b_qtm])
                    for q in range(4):
                        pstt = ps0 if q < 2 else ps1
                        b_pstt = b_ps0 if q < 2 else b_ps1
                        p.op("te", "matmul",
                             pstt[:, (q % 2) * 512:(q % 2 + 1) * 512],
                             ctb2[:], grw[:, q * 512:(q + 1) * 512],
                             start=True, stop=True,
                             reads=[b_ctb2, b_grw], writes=[b_pstt])
                    for half, (pp, bpp) in enumerate([(ps0, b_ps0), (ps1, b_ps1)]):
                        p.raw("dve", lambda v, half=half, pp=pp, j=j: v.tensor_tensor(
                            out=uscr[:, half * 1024:(half + 1) * 1024],
                            in0=pp[:, 0:1024].rearrange("p (h l) -> p h l", l=CH),
                            in1=AP(qtm, j * H + half * 16,
                                   [[npb * H, 128], [1, 16], [0, CH]]),
                            op=ALU.add),
                            reads=[bpp, b_qtm], writes=[b_uscr])
                    p.op("dve", "tensor_scalar_min", uscr[:], uscr[:], CAP,
                         reads=[b_uscr], writes=[b_uscr])
                    p.op("act", "activation", Ld[:], uscr[:], AF.Exp,
                         reads=[b_uscr], writes=[b_Ld])
                    for cc in range(2):
                        p.op("te", "matmul", ps2[cc * 64:(cc + 1) * 64, 0:CH],
                             bf2[:, tp0 - t0 + cc * 64:tp0 - t0 + (cc + 1) * 64],
                             cf2[:, tp0 - t0 + cc * 64:tp0 - t0 + (cc + 1) * 64],
                             start=True, stop=True, tile_position=(0, cc * 64),
                             skip_group_check=(cc == 1),
                             reads=[b_bf2, b_cf2], writes=[b_ps2])
                    p.op("dve", "tensor_tensor", out=gm2[:, 0:CH],
                         in0=ps2[:, 0:CH], in1=ctril[:], op=ALU.mult,
                         reads=[b_ps2, b_ctril], writes=[b_gm2])
                    p.raw("dve", lambda v: v.tensor_tensor(
                        out=M3[:].rearrange("p (h l) -> p h l", l=CH),
                        in0=Ld[:].rearrange("p (h l) -> p h l", l=CH),
                        in1=AP(gm2, 0, [[2 * CH, 128], [0, H], [1, CH]]),
                        op=ALU.mult),
                        reads=[b_Ld, b_gm2], writes=[b_M3])
                    p.op("dve", "tensor_add", out=M3[:], in0=M3[:], in1=ddiag[:],
                         reads=[b_M3, b_ddiag], writes=[b_M3])

                    for cc in range(2):
                        c_glob = tp0 // CH + cc
                        loff = (j * 128 + cc * 64)
                        p.op("act", "activation", Sb_[:], S[:], AF.Identity,
                             reads=[b_S], writes=[b_Sb])
                        for b in range(ZB):
                            p.op("te", "matmul", ps1[:, b * 64:(b + 1) * 64],
                                 Sb_[:, b * 128:(b + 1) * 128],
                                 cf2[:, loff:loff + CH],
                                 start=True, stop=True, skip_group_check=(b > 0),
                                 reads=[b_Sb, b_cf2], writes=[b_ps1])
                        for b in range(ZB):
                            for hh in range(2):
                                p.op("te", "matmul",
                                     ps0[hh * 64:(hh + 1) * 64, b * 64:(b + 1) * 64],
                                     xstm2[cc * 64:(cc + 1) * 64,
                                           j * DI + b * 128 + hh * 64:
                                           j * DI + b * 128 + (hh + 1) * 64],
                                     M3[cc * 64:(cc + 1) * 64,
                                        (2 * b + hh) * CH:(2 * b + hh + 1) * CH],
                                     start=True, stop=True,
                                     tile_position=(cc * 64, hh * 64),
                                     skip_group_check=(b + hh > 0),
                                     reads=[b_xstm2, b_M3], writes=[b_ps0])
                        p.raw("dve", lambda v, loff=loff: v.tensor_tensor(
                            out=yscr[:].rearrange("p (b l) -> p b l", l=CH),
                            in0=ps1[:, 0:1024].rearrange("p (b l) -> p b l", l=CH),
                            in1=AP(erep, loff, [[ZB * TB, 128], [TB, ZB], [1, CH]]),
                            op=ALU.mult),
                            reads=[b_ps1, b_erep], writes=[b_yscr])
                        p.op("dve", "tensor_add", out=yscr[:], in0=yscr[:],
                             in1=ps0[:, 0:1024], reads=[b_yscr, b_ps0],
                             writes=[b_yscr])
                        p.raw("dve", lambda v, loff=loff: v.tensor_tensor(
                            out=yscr[:].rearrange("p (b l) -> p b l", l=CH),
                            in0=yscr[:].rearrange("p (b l) -> p b l", l=CH),
                            in1=AP(silz2, loff, [[ZB * TB, 128], [TB, ZB], [1, CH]]),
                            op=ALU.mult),
                            reads=[b_yscr, b_silz2], writes=[b_yscr])
                        p.op("act", "activation", yg2[:], yscr[:], AF.Square,
                             reads=[b_yscr], writes=[b_yg2])
                        for b in range(ZB):
                            p.op("te", "matmul", ps2[0:1, 512:512 + CH],
                                 coK[:], yg2[:, b * 64:(b + 1) * 64],
                                 start=(b == 0), stop=(b == ZB - 1),
                                 reads=[b_coK, b_yg2], writes=[b_ps2])
                        p.op("dve", "tensor_scalar", out=rowa[0:1, 0:CH],
                             in0=ps2[0:1, 512:512 + CH], scalar1=1.0 / DI,
                             scalar2=EPS, op0=ALU.mult, op1=ALU.add,
                             reads=[b_ps2], writes=[b_rowa])
                        p.op("act", "activation", rowa[0:1, 0:CH],
                             rowa[0:1, 0:CH], AF.Ln,
                             reads=[b_rowa], writes=[b_rowa])
                        p.op("act", "activation", rowa[0:1, 0:CH],
                             rowa[0:1, 0:CH], AF.Exp, scale=-0.5,
                             reads=[b_rowa], writes=[b_rowa])
                        p.op("te", "matmul", ps2[:, 768:768 + CH], co1f[:],
                             rowa[0:1, 0:CH], start=True, stop=True,
                             reads=[b_co1f, b_rowa], writes=[b_ps2])
                        p.raw("dve", lambda v, loff=loff: v.scalar_tensor_tensor(
                            out=AP(yn2, loff, [[ZB * TB, 128], [TB, ZB], [1, CH]]),
                            in0=yscr[:].rearrange("p (b l) -> p b l", l=CH),
                            scalar=1.0,
                            in1=AP(ps2, 768, [[1024, 128], [0, ZB], [1, CH]]),
                            op0=ALU.mult, op1=ALU.mult),
                            reads=[b_yscr, b_ps2], writes=[b_yn2])
                        for q in range(4):
                            pstt = ps0 if q < 2 else ps1
                            b_pstt = b_ps0 if q < 2 else b_ps1
                            p.op("te", "matmul",
                                 pstt[:, (q % 2) * 512:(q % 2 + 1) * 512],
                                 bt2[cc * 64:(cc + 1) * 64, j * DS:(j + 1) * DS],
                                 xsw2[cc * 64:(cc + 1) * 64,
                                      j * DI + q * 512:j * DI + (q + 1) * 512],
                                 start=True, stop=True, tile_position=(cc * 64, 0),
                                 reads=[b_bt2, b_xsw2], writes=[b_pstt])
                        p.raw("te", lambda t, c_glob=c_glob: t.matmul(
                            ps2[:, 0:H], co1f[:],
                            dcrow[0:1, c_glob * H:(c_glob + 1) * H],
                            start=True, stop=True),
                            reads=[b_co1f, b_dcrow], writes=[b_ps2])
                        p.raw("dve", lambda v: v.tensor_tensor(
                            out=St[:], in0=S[:],
                            in1=AP(ps2, 0, [[1024, 128], [1, H], [0, PH]]),
                            op=ALU.mult),
                            reads=[b_S, b_ps2], writes=[b_St])
                        p.raw("dve", lambda v: v.tensor_add(
                            out=S[:, 0:1024], in0=St[:, 0:1024], in1=ps0[:, :]),
                            reads=[b_St, b_ps0], writes=[b_S])
                        p.raw("dve", lambda v: v.tensor_add(
                            out=S[:, 1024:2048], in0=St[:, 1024:2048], in1=ps1[:, :]),
                            reads=[b_St, b_ps1], writes=[b_S])

                for j in range(npb):
                    for kb in range(ZB):
                        for hf in range(2):
                            p.op("te", "matmul",
                                 ps2[:, hf * 512:(hf + 1) * 512],
                                 yn2[:, kb * TB + j * 128:kb * TB + (j + 1) * 128],
                                 wosb[:, kb * D + hf * 512:kb * D + (hf + 1) * 512],
                                 start=(kb == 0), stop=(kb == ZB - 1),
                                 reads=[b_yn2, b_wo], writes=[b_ps2])
                    sums = stat[:, 0:1]
                    ssq = stat[:, 1:2]
                    mu = stat[:, 2:3]
                    ivar = stat[:, 3:4]
                    scr = stat[:, 4:5]
                    p.op("dve", "reduce_sum", sums, ps2[:, 0:D],
                         axis=mybir.AxisListType.X, reads=[b_ps2], writes=[b_stat])
                    p.op("act", "activation", uscr[:, 0:D], ps2[:, 0:D],
                         AF.Square, accum_out=ssq,
                         reads=[b_ps2], writes=[b_uscr, b_stat])
                    p.op("dve", "tensor_scalar_mul", mu, sums, 1.0 / D,
                         reads=[b_stat], writes=[b_stat])
                    p.op("dve", "tensor_mul", out=scr, in0=mu, in1=mu,
                         reads=[b_stat], writes=[b_stat])
                    p.op("dve", "tensor_scalar_mul", ivar, ssq, 1.0 / D,
                         reads=[b_stat], writes=[b_stat])
                    p.op("dve", "tensor_sub", out=ivar, in0=ivar, in1=scr,
                         reads=[b_stat], writes=[b_stat])
                    p.op("dve", "tensor_scalar_add", ivar, ivar, EPS,
                         reads=[b_stat], writes=[b_stat])
                    p.op("act", "activation", ivar, ivar, AF.Ln,
                         reads=[b_stat], writes=[b_stat])
                    p.op("act", "activation", ivar, ivar, AF.Exp, scale=-0.5,
                         reads=[b_stat], writes=[b_stat])
                    p.op("dve", "tensor_scalar", out=uscr[:, 0:D],
                         in0=ps2[:, 0:D], scalar1=mu, scalar2=ivar,
                         op0=ALU.subtract, op1=ALU.mult,
                         reads=[b_ps2, b_stat], writes=[b_uscr])
                    p.op("dve", "tensor_add", out=xt2[:, j * D:(j + 1) * D],
                         in0=xt2[:, j * D:(j + 1) * D], in1=uscr[:, 0:D],
                         reads=[b_uscr, b_xt2], writes=[b_xt2])
                    gt = t0 + j * 128
                    p.raw("sp", lambda s, j=j, gt=gt: s.dma_start(
                        out=sp_x1[gt:gt + 128, :], in_=xt2[:, j * D:(j + 1) * D]),
                        reads=[b_xt2], writes=[bS["sp_x1"]], dma=True)

        p.barrier()
        _esM.close()
        NTT_C = NT // TC
        npc = TC // 128
        with contextlib.ExitStack() as _esC:
            _esF = contextlib.ExitStack()
            cxp = _esC.enter_context(nc.sbuf_tensor("c_xp", [128, 8 * 128], BF16))
            bfv = _esC.enter_context(nc.sbuf_tensor("bfv_sb", [128, 8 * NST * 32], BF16))
            wex = _esC.enter_context(nc.sbuf_tensor("wex", [128, 8 * HD], BF16))
            qcn = _esC.enter_context(nc.sbuf_tensor("qcn", [128, 32 * D], BF16))
            bpre = _esC.enter_context(nc.sbuf_tensor("bpre", [128, 32], F32))
            ubp = _esC.enter_context(nc.sbuf_tensor("ubp", [128, 32], F32))
            x1t = _esC.enter_context(nc.sbuf_tensor("x1t", [128, npc * D], F32))
            x1n = _esC.enter_context(nc.sbuf_tensor("x1n", [128, npc * D], BF16))
            x1f = _esC.enter_context(nc.sbuf_tensor("x1f", [128, 8 * TC], BF16))
            gp = _esC.enter_context(nc.sbuf_tensor("gp", [128, 32 * TC], BF16))
            fscr = _esC.enter_context(nc.sbuf_tensor("fscr", [128, D], F32))
            stg = _esF.enter_context(nc.sbuf_tensor("stg", [128, HD], BF16))
            chn = _esF.enter_context(nc.sbuf_tensor("chn", [128, HD], BF16))
            wblk = _esF.enter_context(nc.sbuf_tensor("wblk", [128, D], BF16))
            wstg = _esF.enter_context(nc.sbuf_tensor("wstg", [128, D], BF16))
            bvec = _esF.enter_context(nc.sbuf_tensor("bvec", [128, 32], BF16))
            vstg = _esF.enter_context(nc.sbuf_tensor("vstg", [128, 32], F32))
            b_bfv, b_wex, b_qcn, b_bpre = Buf("bfv"), Buf("wex"), Buf("qcn"), Buf("bpre")
            b_cxp = Buf("cxp")
            ld(cxp, b_cxp, xpat_h)
            b_stg, b_sscr, b_r0, b_r1 = Buf("stg"), Buf("sscr"), Buf("r0"), Buf("r1")
            b_chn = Buf("chn")
            b_wblk, b_wstg, b_bvec, b_vstg, b_ubp = (
                Buf("wblk"), Buf("wstg"), Buf("bvec"), Buf("vstg"), Buf("ubp"))
            b_x1t, b_x1n, b_x1f, b_gp = Buf("x1t"), Buf("x1n"), Buf("x1f"), Buf("gp")
            assert 32 * TC >= 2 * HD, "gp too small for fold scratch"
            sscr, b_sscr = gp, b_gp
            b_fscr = Buf("fscr")

            p.op("gps", "dma_start", out=bfv[:], in_=bfv_h, writes=[b_bfv], dma=True)
            p.op("gps", "dma_start", out=ubp[:], in_=up_bias_p_h, writes=[b_ubp], dma=True)

            def build_stage(dst, b_dst, vset, s):
                ai, bi = vset
                st = (1 << s) if s < 7 else (1 << (s - 7))
                sti = int(np.log2(st))
                aoff = (ai * NST + s) * 32
                boff = (bi * NST + s) * 32
                p.raw("dve", lambda v: v.tensor_tensor(
                    out=dst[:].rearrange("p (b c) -> p b c", c=128),
                    in0=AP(cxp, 7 * 128, [[8 * 128, 128], [0, 32], [1, 128]]),
                    in1=AP(bfv, aoff, [[8 * NST * 32, 128], [1, 32], [0, 128]]),
                    op=ALU.mult),
                    reads=[b_cxp, b_bfv], writes=[b_dst])
                p.raw("dve", lambda v: v.tensor_tensor(
                    out=sscr[:, 0:HD].rearrange("p (b c) -> p b c", c=128),
                    in0=AP(cxp, sti * 128, [[8 * 128, 128], [0, 32], [1, 128]]),
                    in1=AP(bfv, boff, [[8 * NST * 32, 128], [1, 32], [0, 128]]),
                    op=ALU.mult),
                    reads=[b_cxp, b_bfv], writes=[b_sscr])
                p.op("dve", "tensor_add", out=dst[:], in0=dst[:], in1=sscr[:, 0:HD],
                     reads=[b_dst, b_sscr], writes=[b_dst])

            def chain(dst, b_dst, seed_vset, seed_s, steps):
                r0v = gp[:, HD:2 * HD]
                bufs = ([(r0v, b_gp), (dst, b_dst)] if len(steps) % 2 == 1
                        else [(dst, b_dst), (r0v, b_gp)])
                build_stage(bufs[0][0], bufs[0][1], seed_vset, seed_s)
                cur, b_cur = bufs[0]
                nxt, b_nxt = bufs[1]
                for vset, s in steps:
                    build_stage(stg, b_stg, vset, s)
                    for b in range(32):
                        p.op("te", "matmul", ps0[:, 0:128],
                             stg[:, b * 128:(b + 1) * 128],
                             cur[:, b * 128:(b + 1) * 128],
                             start=True, stop=True, skip_group_check=True,
                             reads=[b_stg, b_cur], writes=[b_ps0])
                        p.op("dve", "tensor_copy",
                             out=nxt[:, b * 128:(b + 1) * 128], in_=ps0[:, 0:128],
                             reads=[b_ps0], writes=[b_nxt])
                    (cur, b_cur), (nxt, b_nxt) = (nxt, b_nxt), (cur, b_cur)

            def dump_chain(tag):
                if "fold" not in dbgset:
                    return
                dd = nc.dram_tensor("dbgc_" + tag, [128, HD], BF16,
                                    kind="ExternalOutput").ap()
                p.op("sp", "dma_start", out=dd, in_=chn[:], reads=[b_chn], dma=True)

            def dump_sb(tag, t, b, shape, dtp):
                if "fold" not in dbgset:
                    return
                dd = nc.dram_tensor("dbgs_" + tag, shape, dtp,
                                    kind="ExternalOutput").ap()
                p.op("sp", "dma_start", out=dd, in_=t[:], reads=[b], dma=True)

            dump_sb("cxp", cxp, b_cxp, [128, 8 * 128], BF16)
            dump_sb("bfv", bfv, b_bfv, [128, 8 * NST * 32], BF16)

            chain(chn, b_chn, (2, 3), 6, [((0, 1), k) for k in [5, 4, 3, 2, 1, 0]])
            dump_chain("bd1uT")
            for blk in range(32):
                p.raw("gps", lambda g, blk=blk: g.dma_start(
                    out=wblk[:, 0:D],
                    in_=w_expT_h[blk * 128:(blk + 1) * 128, :]),
                    writes=[b_wblk], dma=True)
                for hf in range(2):
                    p.op("te", "matmul", ps0[:, hf * 512:(hf + 1) * 512],
                         chn[:, blk * 128:(blk + 1) * 128],
                         wblk[:, hf * 512:(hf + 1) * 512],
                         start=True, stop=True, skip_group_check=True,
                         reads=[b_chn, b_wblk], writes=[b_ps0])
                p.op("dve", "tensor_copy", out=wstg[:], in_=ps0[:, 0:D],
                     reads=[b_ps0], writes=[b_wstg])
                p.raw("sp", lambda s, blk=blk: s.dma_start(
                    out=sp_w[blk * 128:(blk + 1) * 128, :], in_=wstg[:]),
                    reads=[b_wstg], writes=[bS["sp_w"]], dma=True)
            p.op("gps", "dma_start", out=vstg[:], in_=b_exp2_h,
                 writes=[b_vstg], dma=True)
            p.op("dve", "tensor_copy", out=bvec[:], in_=vstg[:],
                 reads=[b_vstg], writes=[b_bvec])
            for blk in range(32):
                p.op("te", "matmul", ps1[:, blk:blk + 1],
                     chn[:, blk * 128:(blk + 1) * 128], bvec[:, blk:blk + 1],
                     start=True, stop=True, skip_group_check=True,
                     reads=[b_chn, b_bvec], writes=[b_ps1])
            p.op("dve", "tensor_copy", out=vstg[:], in_=ps1[:, 0:32],
                 reads=[b_ps1], writes=[b_vstg])
            p.raw("sp", lambda s: s.dma_start(out=sp_bv, in_=vstg[:]),
                  reads=[b_vstg], writes=[bS["sp_bv"]], dma=True)

            chain(chn, b_chn, (2, 3), 11, [((0, 1), k) for k in [10, 9, 8, 7]])
            dump_chain("c2uT")
            for bp in range(32):
                for jq in range(4):
                    p.raw("sp", lambda s, bp=bp, jq=jq: s.dma_start(
                        out=wblk[jq * 32:(jq + 1) * 32, 0:D],
                        in_=AP(sp_w.tensor, (4 * bp + jq) * D,
                               [[128 * D, 32], [1, D]])),
                        reads=[bS["sp_w"]], writes=[b_wblk], dma=True)
                for hf in range(2):
                    p.op("te", "matmul", ps0[:, hf * 512:(hf + 1) * 512],
                         chn[:, bp * 128:(bp + 1) * 128],
                         wblk[:, hf * 512:(hf + 1) * 512],
                         start=True, stop=True, skip_group_check=True,
                         reads=[b_chn, b_wblk], writes=[b_ps0])
                p.op("dve", "tensor_copy", out=wstg[:], in_=ps0[:, 0:D],
                     reads=[b_ps0], writes=[b_wstg])
                p.raw("sp", lambda s, bp=bp: s.dma_start(
                    out=sp_w2[bp * 128:(bp + 1) * 128, :], in_=wstg[:]),
                    reads=[b_wstg], writes=[bS["sp_w2"]], dma=True)
            for k in range(8):
                p.raw("sp", lambda s, k=k: s.dma_start(
                    out=wex[:, k * HD:(k + 1) * HD],
                    in_=sp_w2[:, k * 128:(k + 1) * 128], transpose=True),
                    reads=[bS["sp_w2"]], writes=[b_wex], dma=True)
            for jq in range(4):
                p.raw("sp", lambda s, jq=jq: s.dma_start(
                    out=vstg[jq * 32:(jq + 1) * 32, 0:32],
                    in_=AP(sp_bv.tensor, jq, [[128, 32], [4, 32]]),
                    allow_slow_non_contiguous=True),
                    reads=[bS["sp_bv"]], writes=[b_vstg], dma=True)
            p.op("dve", "tensor_copy", out=bvec[:], in_=vstg[:],
                 reads=[b_vstg], writes=[b_bvec])
            for blk in range(32):
                p.op("te", "matmul", ps1[:, blk:blk + 1],
                     chn[:, blk * 128:(blk + 1) * 128], bvec[:, blk:blk + 1],
                     start=True, stop=True, skip_group_check=True,
                     reads=[b_chn, b_bvec], writes=[b_ps1])
            p.op("dve", "tensor_add", out=bpre[:], in0=ps1[:, 0:32], in1=ubp[:],
                 reads=[b_ps1, b_ubp], writes=[b_bpre])

            chain(chn, b_chn, (4, 5), 7, [((6, 7), k) for k in [8, 9, 10, 11]])
            dump_chain("c2d")
            for blk in range(32):
                p.raw("gps", lambda g, blk=blk: g.dma_start(
                    out=wblk[:, 0:D],
                    in_=w_conP_h[blk * 128:(blk + 1) * 128, :]),
                    writes=[b_wblk], dma=True)
                for hf in range(2):
                    p.op("te", "matmul", ps0[:, hf * 512:(hf + 1) * 512],
                         chn[:, blk * 128:(blk + 1) * 128],
                         wblk[:, hf * 512:(hf + 1) * 512],
                         start=True, stop=True, skip_group_check=True,
                         reads=[b_chn, b_wblk], writes=[b_ps0])
                p.op("dve", "tensor_copy", out=wstg[:], in_=ps0[:, 0:D],
                     reads=[b_ps0], writes=[b_wstg])
                p.raw("sp", lambda s, blk=blk: s.dma_start(
                    out=sp_w[blk * 128:(blk + 1) * 128, :], in_=wstg[:]),
                    reads=[b_wstg], writes=[bS["sp_w"]], dma=True)

            chain(chn, b_chn, (4, 5), 0, [((6, 7), k) for k in [1, 2, 3, 4, 5, 6]])
            dump_chain("bd1d")
            for blk in range(32):
                p.raw("sp", lambda s, blk=blk: s.dma_start(
                    out=wblk[:, 0:D],
                    in_=AP(sp_w.tensor, blk * D, [[32 * D, 128], [1, D]])),
                    reads=[bS["sp_w"]], writes=[b_wblk], dma=True)
                for hf in range(2):
                    p.op("te", "matmul", ps0[:, hf * 512:(hf + 1) * 512],
                         chn[:, blk * 128:(blk + 1) * 128],
                         wblk[:, hf * 512:(hf + 1) * 512],
                         start=True, stop=True, skip_group_check=True,
                         reads=[b_chn, b_wblk], writes=[b_ps0])
                p.op("dve", "tensor_copy", out=wstg[:], in_=ps0[:, 0:D],
                     reads=[b_ps0], writes=[b_wstg])
                p.raw("sp", lambda s, blk=blk: s.dma_start(
                    out=sp_w2[blk * 128:(blk + 1) * 128, :], in_=wstg[:]),
                    reads=[b_wstg], writes=[bS["sp_w2"]], dma=True)
            for bp in range(32):
                for jq in range(4):
                    p.raw("sp", lambda s, bp=bp, jq=jq: s.dma_start(
                        out=qcn[jq * 32:(jq + 1) * 32, bp * D:(bp + 1) * D],
                        in_=AP(sp_w2.tensor, (4 * bp + jq) * D,
                               [[128 * D, 32], [1, D]])),
                        reads=[bS["sp_w2"]], writes=[b_qcn], dma=True)

            _esF.close()
            for it in range(NTT_C):
                t0 = it * TC
                for j in range(npc):
                    gt = t0 + j * 128
                    p.raw("sp", lambda s, j=j, gt=gt: s.dma_start(
                        out=x1t[:, j * D:(j + 1) * D], in_=sp_x1[gt:gt + 128, :]),
                        reads=[bS["sp_x1"]], writes=[b_x1t], dma=True)
                for j in range(npc):
                    ln_tm(x1t, b_x1t, x1n, b_x1n, 128, D, j * D)
                for j in range(npc):
                    for k in range(8):
                        pt, b_pt = PT[(j * 8 + k) % 2]
                        p.op("te", "transpose", pt[:, 0:128],
                             x1n[:, j * D + k * 128:j * D + (k + 1) * 128],
                             cid[:], reads=[b_x1n, b_cid], writes=[b_pt])
                        p.op("dve", "tensor_copy",
                             out=x1f[:, k * TC + j * 128:k * TC + (j + 1) * 128],
                             in_=pt[:, 0:128], reads=[b_pt], writes=[b_x1f])
                slots = [(ps0, b_ps0, 0), (ps0, b_ps0, 512),
                         (ps2, b_ps2, 0), (ps2, b_ps2, 512)]
                si = 0
                for fb in range(32):
                    psx, b_psx, off = slots[si]; si = (si + 1) % 4
                    for k in range(8):
                        p.op("te", "matmul", psx[:, off:off + TC],
                             wex[:, k * HD + fb * 128:k * HD + (fb + 1) * 128],
                             x1f[:, k * TC:(k + 1) * TC],
                             start=(k == 0), stop=(k == 7), skip_group_check=True,
                             reads=[b_wex, b_x1f], writes=[b_psx])
                    p.op("act", "activation", gp[:, fb * TC:(fb + 1) * TC],
                         psx[:, off:off + TC], AF.Gelu, bias=bpre[:, fb:fb + 1],
                         reads=[b_psx, b_bpre], writes=[b_gp])
                for j in range(npc):
                    for fb in range(32):
                        for hf in range(2):
                            p.op("te", "matmul", ps1[:, hf * 512:(hf + 1) * 512],
                                 gp[:, fb * TC + j * 128:fb * TC + (j + 1) * 128],
                                 qcn[:, fb * D + hf * 512:fb * D + (hf + 1) * 512],
                                 start=(fb == 0), stop=False, skip_group_check=True,
                                 reads=[b_gp, b_qcn], writes=[b_ps1])
                    for hf in range(2):
                        p.op("te", "matmul", ps1[:, hf * 512:(hf + 1) * 512],
                             co1[:], boutsb[0:1, hf * 512:(hf + 1) * 512],
                             start=False, stop=True, skip_group_check=True,
                             reads=[b_co1, b_bout], writes=[b_ps1])
                    sums = stat[:, 0:1]
                    ssq = stat[:, 1:2]
                    mu = stat[:, 2:3]
                    ivar = stat[:, 3:4]
                    scr = stat[:, 4:5]
                    p.op("dve", "reduce_sum", sums, ps1[:, 0:D],
                         axis=mybir.AxisListType.X, reads=[b_ps1], writes=[b_stat])
                    p.op("act", "activation", fscr[:], ps1[:, 0:D],
                         AF.Square, accum_out=ssq,
                         reads=[b_ps1], writes=[b_fscr, b_stat])
                    p.op("dve", "tensor_scalar_mul", mu, sums, 1.0 / D,
                         reads=[b_stat], writes=[b_stat])
                    p.op("dve", "tensor_mul", out=scr, in0=mu, in1=mu,
                         reads=[b_stat], writes=[b_stat])
                    p.op("dve", "tensor_scalar_mul", ivar, ssq, 1.0 / D,
                         reads=[b_stat], writes=[b_stat])
                    p.op("dve", "tensor_sub", out=ivar, in0=ivar, in1=scr,
                         reads=[b_stat], writes=[b_stat])
                    p.op("dve", "tensor_scalar_add", ivar, ivar, EPS,
                         reads=[b_stat], writes=[b_stat])
                    p.op("act", "activation", ivar, ivar, AF.Ln,
                         reads=[b_stat], writes=[b_stat])
                    p.op("act", "activation", ivar, ivar, AF.Exp, scale=-0.5,
                         reads=[b_stat], writes=[b_stat])
                    p.op("dve", "tensor_scalar", out=fscr[:],
                         in0=ps1[:, 0:D], scalar1=mu, scalar2=ivar,
                         op0=ALU.subtract, op1=ALU.mult,
                         reads=[b_ps1, b_stat], writes=[b_fscr])
                    p.op("dve", "tensor_add", out=fscr[:],
                         in0=fscr[:], in1=x1t[:, j * D:(j + 1) * D],
                         reads=[b_fscr, b_x1t], writes=[b_fscr])
                    gt = t0 + j * 128
                    p.raw("sp", lambda s, gt=gt: s.dma_start(
                        out=out_ap[gt:gt + 128, :], in_=fscr[:]),
                        reads=[b_fscr], dma=True)

        for dn in dbgset - {"fold"}:
            dd = nc.dram_tensor("dbg_" + dn, list(locals()[dn].tensor.shape),
                                locals()[dn].tensor.dtype,
                                kind="ExternalOutput").ap()
            p.op("sp", "dma_start", out=dd, in_=locals()[dn],
                 reads=[bS[dn]] if dn in bS else [], dma=True)
        p.fence("sp", [(s, p.cnt[s]) for s in p.cnt if s.startswith("D")])
    return nc


def _perm_f2fp():
    f = np.arange(HD)
    return (f % 128) * 32 + f // 128


def _bf_vectors(weights):
    P = _perm_f2fp()
    a = np.zeros((NST, HD), np.float32)
    b = np.zeros((NST, HD), np.float32)
    aT = np.zeros((NST, HD), np.float32)
    bT = np.zeros((NST, HD), np.float32)
    f = np.arange(HD)
    for s in range(NST):
        st = 1 << s
        nb = HD // (2 * st)
        w = weights[s].reshape(nb, st, 2, 2)
        blk = f // (2 * st)
        hi = (f // st) % 2
        r = f % st
        av = np.where(hi == 0, w[blk, r, 0, 0], w[blk, r, 1, 1]).astype(np.float32)
        bv = np.where(hi == 0, w[blk, r, 0, 1], w[blk, r, 1, 0]).astype(np.float32)
        if s < 7:
            a[s] = av
            b[s] = bv
            aT[s] = av
            bT[s] = bv[f ^ st]
        else:
            avp = np.empty(HD, np.float32)
            bvp = np.empty(HD, np.float32)
            avp[P[f]] = av
            bvp[P[f]] = bv
            a[s] = avp
            b[s] = bvp
            aT[s] = avp
            stp = 1 << (s - 7)
            bT[s] = bvp[np.arange(HD) ^ stp]
    return a, b, aT, bT


def _fm_blocks(vec):
    return np.ascontiguousarray(vec.reshape(-1, 128).T)


def _vec_layout(vs):
    out = np.zeros((128, NST * 32), np.float32)
    for s in range(NST):
        out[:, s * 32:(s + 1) * 32] = vs[s].reshape(32, 128).T
    return out


_NC_CACHE = {}


def kernel(L0_attn_x, L0_attn_W_in, L0_attn_conv_w, L0_attn_conv_b,
           L0_attn_dt_bias, L0_attn_A_log, L0_attn_D, L0_attn_norm_w,
           L0_attn_W_out, L0_ffn_W_exp, L0_ffn_b_exp, L0_ffn_up_weights,
           L0_ffn_up_bias, L0_ffn_down_weights, L0_ffn_W_con, L0_ffn_b_con,
           L0_ffn_down_bias):
    import ml_dtypes
    x = np.ascontiguousarray(np.asarray(L0_attn_x, dtype=np.float32))
    B, L, _ = x.shape
    NT = (B * L) // NC_
    half = L // 2

    key = ("fused", NT)
    if key not in _NC_CACHE:
        _NC_CACHE[key] = build_core_program(NT=NT, TA=256, TB=256, TC=256,
                                            n_cores=NC_, exchange="cc")
    nc = _NC_CACHE[key]

    f32 = np.float32
    bf16 = ml_dtypes.bfloat16
    w_in = np.ascontiguousarray(np.asarray(L0_attn_W_in, f32))
    conv_w = np.asarray(L0_attn_conv_w, f32).reshape(KC, 2304)
    conv_w_l = np.zeros((128, KC * XB), f32)
    for k in range(KC):
        conv_w_l[:, k * XB:(k + 1) * XB] = conv_w[k].reshape(XB, 128).T
    conv_b_l = np.asarray(L0_attn_conv_b, f32).reshape(XB, 128).T.copy()
    dt_bias = np.asarray(L0_attn_dt_bias, f32).reshape(H, 1)
    a_log = np.asarray(L0_attn_A_log, f32).reshape(H, 1)
    dvec = np.asarray(L0_attn_D, f32).reshape(1, H)
    norm_w_l = np.asarray(L0_attn_norm_w, f32).reshape(ZB, 128).T.copy()
    w_out = np.ascontiguousarray(np.asarray(L0_attn_W_out, f32))
    P = _perm_f2fp()
    w_expT = np.ascontiguousarray(np.asarray(L0_ffn_W_exp, f32).T)
    w_conP = np.zeros((HD, D), f32)
    w_conP[P] = np.asarray(L0_ffn_W_con, f32)
    b_exp2 = _fm_blocks(np.asarray(L0_ffn_b_exp, f32))
    up_bias_p = np.zeros(HD, f32)
    up_bias_p[P] = np.asarray(L0_ffn_up_bias, f32)
    up_bias_p = _fm_blocks(up_bias_p)
    b_out = np.asarray(L0_ffn_b_con, f32).reshape(1, D)
    down_b = np.asarray(L0_ffn_down_bias, f32).reshape(1, D)
    ua, ub, uaT, ubT = _bf_vectors(np.asarray(L0_ffn_up_weights, f32))
    da, db, daT, dbT = _bf_vectors(np.asarray(L0_ffn_down_weights, f32))
    bfv = np.concatenate([_vec_layout(v) for v in
                          (ua, ub, uaT, ubT, da, db, daT, dbT)],
                         axis=1).astype(bf16)
    ident = np.eye(128, dtype=bf16)
    xpat = np.zeros((128, 8 * 128), bf16)
    for sti in range(7):
        st = 1 << sti
        idx = np.arange(128)
        m = np.zeros((128, 128), f32)
        m[idx, idx ^ st] = 1.0
        xpat[:, sti * 128:(sti + 1) * 128] = m.astype(bf16)
    xpat[:, 7 * 128:8 * 128] = np.eye(128, dtype=bf16)
    tril = np.zeros((128, CH), f32)
    ss = np.arange(128) % 64
    tril[:, :] = (ss[:, None] <= np.arange(CH)[None, :]).astype(f32)
    tb2 = np.zeros((2, 128), f32)
    tb2[0, 0:64] = 1.0
    tb2[1, 64:128] = 1.0
    ohE = np.zeros((H, 16 * 128), f32)
    for b in range(16):
        ohE[2 * b, b * 128:b * 128 + 64] = 1.0
        ohE[2 * b + 1, b * 128 + 64:(b + 1) * 128] = 1.0

    shared = dict(w_in=w_in, conv_w=conv_w_l, conv_b=conv_b_l,
                  dt_bias=dt_bias, a_log=a_log, dvec=dvec, norm_w=norm_w_l,
                  w_out=w_out, w_expT=w_expT, w_conP=w_conP, b_exp2=b_exp2,
                  up_bias_p=up_bias_p, b_out=b_out, down_b=down_b, bfv=bfv,
                  ident=np.asarray(ident), xpat=np.asarray(xpat), tril=tril,
                  tb2=np.asarray(tb2), ohE=np.asarray(ohE))

    in_maps = []
    for c in range(NC_):
        seq = c // 2
        hi = c % 2
        xs = x[seq, hi * half:(hi + 1) * half, :]
        if hi == 0:
            xh = np.zeros((4, D), f32)
        else:
            xh = np.zeros((4, D), f32)
            xh[0:3] = x[seq, half - 3:half, :]
        smask = np.zeros((128, 8), f32)
        if hi == 1:
            smask[:, c - 1] = 1.0
        m = dict(shared)
        m["x"] = np.ascontiguousarray(xs)
        m["xh"] = xh
        m["smask"] = smask
        in_maps.append(m)

    res = run_bass_kernel_spmd(nc, in_maps, core_ids=list(range(NC_)))
    out = np.empty((B, L, D), np.float32)
    for c in range(NC_):
        seq = c // 2
        hi = c % 2
        out[seq, hi * half:(hi + 1) * half, :] = res.results[c]["out"]
    return out
